# version 02
# baseline (speedup 1.0000x reference)
import contextlib

import numpy as np
import ml_dtypes
import concourse.bass as bass
import concourse.mybir as mybir
from concourse.bass_utils import run_bass_kernel_spmd

F32 = mybir.dt.float32
BF16 = mybir.dt.bfloat16
AF = mybir.ActivationFunctionType
ALU = mybir.AluOpType
AX = mybir.AxisListType
BF = ml_dtypes.bfloat16

NCORES = 8
T = 2048
D = 1024
C = 8
TGS = 512
NTG = T // TGS
NTT = T // 128
NEG = -3.0e38

ENGS = ("T", "V", "S", "G", "Q")


class KB:
    def __init__(self, nc, n_dma_channels=8):
        self.nc = nc
        self.stack = contextlib.ExitStack()
        self.prog = {e: [] for e in ENGS}
        self.cnt = {e: 0 for e in ENGS}
        self.sem = {}
        for e in ENGS:
            self.sem[e] = self.stack.enter_context(nc.semaphore("sem_" + e))
        self.dma_cnt = {}
        for i in range(n_dma_channels):
            ch = "D%d" % i
            self.sem[ch] = self.stack.enter_context(nc.semaphore("sem_" + ch))
            self.dma_cnt[ch] = 0
        self._chfree = ["D%d" % i for i in range(n_dma_channels)]
        self.hwm = {}
        self.regions = {}

    def channel(self):
        return self._chfree.pop(0)

    def _emit_wait(self, e, p, v):
        if p in self.dma_cnt:
            v = self.dma_cnt[p]
        if self.hwm.get((e, p), 0) >= v:
            return
        self.hwm[(e, p)] = v
        sem = self.sem[p]
        self.prog[e].append(lambda eng, sem=sem, v=v: eng.wait_ge(sem, v))

    def wait(self, e, *tokens):
        best = {}
        for tok in tokens:
            if tok is None:
                continue
            p, v = tok
            if v > best.get(p, 0):
                best[p] = v
        for p, v in best.items():
            self._emit_wait(e, p, v)

    def _raw_op(self, e, fn, inc):
        if inc:
            self.cnt[e] += 1
            v = self.cnt[e]
            sem = self.sem[e]
            self.prog[e].append(lambda eng, fn=fn, sem=sem: fn(eng).then_inc(sem, 1))
            return (e, v)
        self.prog[e].append(fn)
        return None

    def _rel(self, name):
        out = []
        for r in self.regions:
            if r == name or r.startswith(name + "/") or name.startswith(r + "/"):
                out.append(r)
        return out

    def _read_tokens(self, names):
        toks = []
        for n in names:
            for r in self._rel(n):
                toks.append(self.regions[r][0])
        return toks

    def _write_tokens(self, names):
        toks = []
        for n in names:
            for r in self._rel(n):
                w, rd = self.regions[r]
                toks.append(w)
                toks.extend(rd)
        return toks

    def _note_read(self, names, tok):
        for n in names:
            self.regions.setdefault(n, (None, []))[1].append(tok)

    def _note_write(self, names, tok):
        for n in names:
            for r in self._rel(n):
                if r != n:
                    del self.regions[r]
            self.regions[n] = (tok, [])

    def do(self, e, fn, reads=(), writes=(), extra=(), inc=True):
        toks = self._read_tokens(reads) + self._write_tokens(writes) + list(extra)
        self.wait(e, *toks)
        tok = self._raw_op(e, fn, inc)
        if tok is not None:
            self._note_read(reads, tok)
            self._note_write(writes, tok)
        return tok

    def dma(self, ch, qe, out_ap, in_ap, reads=(), writes=(), extra=(), accum=False):
        toks = self._read_tokens(reads) + self._write_tokens(writes) + list(extra)
        self.wait(qe, *toks)
        if self.dma_cnt[ch] > 0:
            self._emit_wait(qe, ch, self.dma_cnt[ch])
        self.dma_cnt[ch] += 16
        v = self.dma_cnt[ch]
        sem = self.sem[ch]
        if accum:
            self.prog[qe].append(
                lambda eng, o=out_ap, i=in_ap, sem=sem:
                    eng.dma_start(o, i, accum_op=ALU.add).then_inc(sem, 16))
        else:
            self.prog[qe].append(
                lambda eng, o=out_ap, i=in_ap, sem=sem:
                    eng.dma_start(o, i).then_inc(sem, 16))
        tok = (ch, v)
        self._note_read(reads, tok)
        self._note_write(writes, tok)
        return tok

    def mm(self, mms, reads=(), writes=(), inc=True):
        toks = self._read_tokens(reads) + self._write_tokens(writes)
        self.wait("T", *toks)
        tok = None
        for i, m in enumerate(mms):
            out_ap, lhsT, rhs, start, stop = m[:5]
            kw = m[5] if len(m) > 5 else {}
            last = inc and (i == len(mms) - 1)
            t2 = self._raw_op(
                "T",
                lambda eng, o=out_ap, l=lhsT, r=rhs, s0=start, s1=stop, kw=kw:
                    eng.matmul(o, l, r, start=s0, stop=s1,
                               skip_group_check=True, **kw),
                last)
            if last:
                tok = t2
        self._note_read(reads, tok)
        self._note_write(writes, tok)
        return tok

    def retag(self, name):
        toks = self._write_tokens([name])
        for r in self._rel(name):
            del self.regions[r]
        self.regions[name] = (None, [t for t in toks if t is not None])

    def token(self, e):
        return (e, self.cnt[e])

    def build(self):
        nc = self.nc
        with nc.Block() as block:
            for e, starter in (("Q", block.sync), ("G", block.gpsimd),
                               ("T", block.tensor), ("V", block.vector),
                               ("S", block.scalar)):
                fns = self.prog[e]
                if not fns:
                    continue

                def body(eng, fns=fns):
                    for f in fns:
                        f(eng)
                starter(body)


def _fm(w):
    d, f = w.shape
    c = (d + 127) // 128
    if d % 128:
        wp = np.zeros((c * 128, f), w.dtype)
        wp[:d] = w
        w = wp
    return np.ascontiguousarray(w.reshape(c, 128, f).transpose(1, 0, 2))


def _pp(v):
    d = v.shape[-1]
    c = (d + 127) // 128
    if d % 128:
        vp = np.zeros(v.shape[:-1] + (c * 128,), v.dtype)
        vp[..., :d] = v
        v = vp
    return np.ascontiguousarray(
        np.moveaxis(v.reshape(v.shape[:-1] + (c, 128)), -1, 0))


def _swap_cols_per_head(w, dh=64):
    f = w.shape[-1]
    idx = np.arange(f)
    h = idx // dh
    r = idx % dh
    perm = h * dh + (r + dh // 2) % dh
    return w[..., perm]


def _interleave128(a, b):
    d, n = a.shape
    k = n // 128
    out = np.empty((d, 2 * n), a.dtype)
    for i in range(k):
        out[:, 256 * i:256 * i + 128] = a[:, 128 * i:128 * i + 128]
        out[:, 256 * i + 128:256 * i + 256] = b[:, 128 * i:128 * i + 128]
    return out


def _rope_maps_l0():
    dh, window = 64, 512
    half_idx = np.arange(0, dh, 2, dtype=np.float64)
    inv_freq = 1.0 / 10000.0 ** (half_idx / dh)
    t = np.arange(T, dtype=np.float64)
    freqs = t[:, None] * inv_freq[None, :]
    freqs = np.concatenate([freqs, freqs], -1)
    base = (half_idx + 0.4 * dh) / (1.4 * dh)
    power = (t - T // 2) / float(window // 2)
    scale = base[None, :] ** power[:, None]
    scale = np.concatenate([scale, scale], -1)
    cos = np.cos(freqs)
    sin = np.sin(freqs)
    sgn = np.where(np.arange(dh) < dh // 2, -1.0, 1.0)
    aq = (cos * scale * 8.0).T
    bq = (sgn[None, :] * sin * scale * 8.0).T
    ak = (cos / scale).T
    bk = (sgn[None, :] * sin / scale).T
    return [np.ascontiguousarray(np.tile(m, (2, 1)).astype(np.float32))
            for m in (aq, bq, ak, bk)]


def _rope_maps_l1f():
    half = 32
    inv = 1.0 / 10000.0 ** (np.arange(half, dtype=np.float64) / half)
    ang = np.arange(T, dtype=np.float64)[:, None] * inv[None, :]
    cos = np.cos(ang)
    sin = np.sin(ang)
    a = np.concatenate([cos, cos], -1)
    b = np.concatenate([-sin, sin], -1)
    return [np.ascontiguousarray(np.tile(m, (2, 1)).astype(np.float32))
            for m in ((a.T * 0.125), (b.T * 0.125), a.T, b.T)]


def _gumbel_noise():
    import jax
    cpu = jax.devices("cpu")[0]
    with jax.default_device(cpu):
        import jax.numpy as jnp
        n = jax.random.gumbel(jax.random.key(42), (8, T, 8), jnp.float32) * 0.05
        return np.asarray(n)


def _stage(inputs):
    f32 = np.float32
    sh = {}
    x = np.asarray(inputs["L0_attn_x"], f32)
    percore = []
    noise = _gumbel_noise()
    for b in range(NCORES):
        xc = np.ascontiguousarray(x[b].T)
        percore.append({
            "x_st": _fm(xc),
            "noise_st": np.ascontiguousarray(
                noise[b].reshape(NTT, 128, 8).transpose(1, 0, 2).astype(f32)),
        })

    sh["c_ones"] = np.ones((128, 128), f32)
    o64 = np.zeros((128, 128), f32)
    o64[:64, :64] = 1.0
    o64[64:, 64:] = 1.0
    sh["c_o64"] = o64
    sh["c_id"] = np.eye(128, dtype=f32)
    md = np.where(np.arange(128)[None, :] <= np.arange(128)[:, None], 0.0, NEG)
    sh["c_maskd"] = md.astype(f32)
    j = np.arange(640)[None, :]
    p = np.arange(128)[:, None]
    ml = np.where((j >= p) & (j <= p + 512), 0.0, NEG)
    sh["c_maskl"] = ml.astype(f32)
    sel = np.zeros((8, 8 * 128), f32)
    for e in range(8):
        sel[e, e * 128:(e + 1) * 128] = 1.0
    sh["c_sel"] = sel
    sh["c_id16"] = (np.eye(128) / 16.0).astype(f32)
    r0 = _rope_maps_l0()
    sh["r0aq"], sh["r0bq"], sh["r0ak"], sh["r0bk"] = r0
    r1 = _rope_maps_l1f()
    sh["r1aq"], sh["r1bq"], sh["r1ak"], sh["r1bk"] = r1
    pos = np.arange(T, dtype=f32)
    rel = np.exp(-np.abs(pos[:, None] - pos[None, :]) / 16.0).astype(f32)
    sh["rel_st"] = np.ascontiguousarray(rel.reshape(NTT, 128, T))

    wqkv = np.asarray(inputs["L0_ffn_wqkv"], f32)
    parts = []
    for l in range(2):
        q = wqkv[l][:, 0:512]
        k = wqkv[l][:, 512:1024]
        v = wqkv[l][:, 1024:1536]
        qx = _interleave128(q, _swap_cols_per_head(q))
        kx = _interleave128(k, _swap_cols_per_head(k))
        parts.append(_fm(np.concatenate([qx, kx, v], -1)))
    sh["w_l0_qkv"] = np.stack(parts)
    sh["w_l0_wo"] = np.stack([_fm(np.asarray(inputs["L0_ffn_wo"], f32)[l])
                              for l in range(2)])
    w1 = np.asarray(inputs["L0_ffn_w1"], f32)
    sh["w_l0_w1a"] = np.stack([_fm(w1[l][:, :2730]) for l in range(2)])
    sh["w_l0_w1g"] = np.stack([_fm(w1[l][:, 2730:]) for l in range(2)])
    w2 = np.asarray(inputs["L0_ffn_w2"], f32)
    sh["w_l0_w2"] = np.stack([_fm(w2[l]) for l in range(2)])
    qs = np.asarray(inputs["L0_ffn_q_scale"], f32)
    ks = np.asarray(inputs["L0_ffn_k_scale"], f32)
    qst = np.tile(qs, (1, 2)).T
    kst = np.tile(ks, (1, 2)).T
    swi = np.concatenate([np.arange(32, 64), np.arange(0, 32),
                          np.arange(96, 128), np.arange(64, 96)])
    sh["p_l0_qs"] = np.stack([qst, qst[swi]], -1).astype(f32)
    sh["p_l0_ks"] = np.stack([kst, kst[swi]], -1).astype(f32)
    sh["p_l0_lng"] = np.stack([_pp(np.asarray(inputs["L0_ffn_ln_g"], f32)[l])
                               for l in range(2)], 1)
    sh["p_l0_lnb"] = np.stack([_pp(np.asarray(inputs["L0_ffn_ln_b"], f32)[l])
                               for l in range(2)], 1)
    b1 = np.asarray(inputs["L0_ffn_b1"], f32)
    sh["p_l0_b1a"] = np.stack([_pp(b1[l][:2730]) for l in range(2)], 1)
    sh["p_l0_b1g"] = np.stack([_pp(b1[l][2730:]) for l in range(2)], 1)
    sh["p_l0_b2"] = np.stack([_pp(np.asarray(inputs["L0_ffn_b2"], f32)[l])
                              for l in range(2)], 1)

    sh["w_l0a_q"] = _fm(np.ascontiguousarray(np.asarray(inputs["L0_attn_Wq"], f32).T))
    sh["w_l0a_o"] = _fm(np.ascontiguousarray(np.asarray(inputs["L0_attn_Wo"], f32).T))
    sh["p_l0a_bo"] = _pp(np.asarray(inputs["L0_attn_bo"], f32))

    wq = np.asarray(inputs["L1_ffn_wq"], f32)
    wk = np.asarray(inputs["L1_ffn_wk"], f32)
    sh["w_l1f_q"] = _fm(_interleave128(wq, _swap_cols_per_head(wq)))
    sh["w_l1f_k"] = _fm(_interleave128(wk, _swap_cols_per_head(wk)))
    sh["w_l1f_v"] = _fm(np.asarray(inputs["L1_ffn_wv"], f32))
    sh["w_l1f_o"] = _fm(np.asarray(inputs["L1_ffn_wo"], f32))
    sh["w_l1f_r"] = _fm(np.asarray(inputs["L1_ffn_router_w"], f32))
    for nm, key in (("e1", "L1_ffn_w1"), ("e2", "L1_ffn_w2"), ("eo", "L1_ffn_wout")):
        w = np.asarray(inputs[key], f32)
        sh["w_l1f_" + nm] = np.stack([_fm(w[e]) for e in range(8)])
    sh["p_l1f_s1"] = _pp(np.asarray(inputs["L1_ffn_scale1"], f32))
    sh["p_l1f_s2"] = _pp(np.asarray(inputs["L1_ffn_scale2"], f32))
    sh["p_l1f_b1"] = np.stack([_pp(np.asarray(inputs["L1_ffn_b1"], f32)[e])
                               for e in range(8)], 1)
    sh["p_l1f_b2"] = np.stack([_pp(np.asarray(inputs["L1_ffn_b2"], f32)[e])
                               for e in range(8)], 1)
    sh["w_l1f_bo"] = np.asarray(inputs["L1_ffn_bout"], f32)

    for nm, key in (("q", "L1_attn_Wq"), ("k", "L1_attn_Wk"),
                    ("1", "L1_attn_W1"), ("2", "L1_attn_W2")):
        w = np.asarray(inputs[key], f32)
        sh["w_l1a_" + nm] = np.stack(
            [_fm(np.ascontiguousarray(w[l].T)) for l in range(6)])
    for nm, key in (("bq", "L1_attn_bq"), ("bk", "L1_attn_bk"),
                    ("l1w", "L1_attn_ln1_w"), ("l1b", "L1_attn_ln1_b"),
                    ("l2w", "L1_attn_ln2_w"), ("l2b", "L1_attn_ln2_b"),
                    ("l3w", "L1_attn_ln3_w"), ("l3b", "L1_attn_ln3_b"),
                    ("b2", "L1_attn_b2")):
        v = np.asarray(inputs[key], f32)
        sh["p_l1a_" + nm] = np.stack([_pp(v[l]) for l in range(6)], 1)
    b1 = np.asarray(inputs["L1_attn_b1"], f32)
    sh["p_l1a_b1"] = np.stack([_pp(b1[l]) for l in range(6)], 1)
    return sh, percore


class Prog:
    def __init__(self, nc, nblocks):
        self.nc = nc
        self.kb = KB(nc)
        self.nblocks = nblocks
        self.ios = {}

    def build(self, shapes):
        nc = self.nc
        kb = self.kb
        st = kb.stack

        for name, arr in shapes.items():
            self.ios[name] = nc.dram_tensor(name, list(arr.shape), F32,
                                            kind="ExternalInput")
        x_io = nc.dram_tensor("x_out", [128, C, T], F32, kind="ExternalOutput")
        self.x_io = x_io

        def sb(name, shape, dt=F32):
            return st.enter_context(nc.sbuf_tensor(name, shape, dt))

        self.ST = sb("ST", [128, C, T], F32)
        self.XB = sb("XB", [128, 16384], BF16)
        self.YB = sb("YB", [128, 16384], BF16)
        self.ZB = sb("ZB", [128, 16384], BF16)
        self.WAR = sb("WAR", [128, 8192], BF16)
        self.PL = sb("PL", [128, 2048], F32)
        self.ATR = sb("ATR", [128, 2048], BF16)
        self.SML = sb("SML", [128, 128], F32)
        self.SQB = sb("SQB", [128, 2, 512], BF16)
        self.SPR = sb("SPR", [128, 256], F32)
        self.MAPS = sb("MAPS", [128, 2, T], BF16)
        self.GRW = self.MAPS[0:8, 0, :]
        self.BOUT = self.MAPS[0:8, 1, 0:1024]
        self.CONST = sb("CONST", [128, 2176], BF16)
        self.CID = sb("CID", [128, 128], F32)
        self.CF = sb("CF", [128, 4], F32)
        self.PSF = st.enter_context(nc.psum_tensor("PSF", [128, 8, 512], F32))

        self.SC0 = self.PL[:, 0:512]
        self.SC1 = self.PL[:, 512:1024]
        self.STAm = self.PL[:, 1024:1536]
        self.STAr = self.PL[:, 1536:2048]
        self.AUS = self.PL[:, 0:2048]

        self._wchs = [kb.channel() for _ in range(4)]
        self._wi = 0
        self.cX = kb.channel()
        self.cR = kb.channel()
        self.cA = kb.channel()

        self._load_consts()
        kb.do("V", lambda v: v.memset(self.CF[:, 0:1], 1e-5), writes=("CF",))
        kb.do("V", lambda v: v.memset(self.CF[:, 1:2], 1e-6), writes=("CF",))
        kb.do("V", lambda v: v.memset(self.CF[:, 2:3], 1e-24), writes=("CF",))
        kb.dma(self.cX, "Q", x_io.ap(), self.ios["x_st"].ap(), writes=("XIO",))

        blocks = [
            (self.b_l0ffn, 0.5), (self.b_l0attn, 1.0), (self.b_l0ffn, 0.5),
            (self.b_l1ffn, 0.5), (self.b_l1attn, 1.0), (self.b_l1ffn, 0.5),
        ]
        for bi in range(self.nblocks):
            fn, sc = blocks[bi]
            self.tln(bi)
            fn(bi)
            self.residual_out(sc)

        tok = kb._write_tokens(["XIO"]) + kb._read_tokens(["XIO"])
        kb.wait("Q", *tok)
        kb.build()

    @property
    def cW(self):
        ch = self._wchs[self._wi % 4]
        self._wi += 1
        return ch

    def _load_consts(self):
        kb = self.kb
        io = self.ios
        CB = self.CONST
        kb.dma(self.cR, "Q", self.CID[:], io["c_id"].ap(), writes=("CID",))
        kb.dma(self.cW, "G", CB[:, 0:128], io["c_ones"].ap(), writes=("CST/ones",))
        kb.dma(self.cW, "G", CB[:, 128:256], io["c_o64"].ap(), writes=("CST/o64",))
        kb.dma(self.cW, "G", CB[:, 256:384], io["c_maskd"].ap(), writes=("CST/maskd",))
        kb.dma(self.cW, "G", CB[:, 384:1024], io["c_maskl"].ap(), writes=("CST/maskl",))
        kb.dma(self.cW, "G", CB[0:8, 1024:2048], io["c_sel"].ap(), writes=("CST/sel",))
        kb.dma(self.cW, "G", CB[:, 2048:2176], io["c_id16"].ap(), writes=("CST/id16",))

    def c_ones(self):
        return self.CONST[:, 0:128]

    def c_o64(self):
        return self.CONST[:, 128:256]

    def c_maskd(self):
        return self.CONST[:, 256:384]

    def c_maskl(self):
        return self.CONST[:, 384:1024]

    def c_sel(self, e):
        return self.CONST[0:8, 1024 + e * 128:1024 + (e + 1) * 128]

    def c_id16(self):
        return self.CONST[:, 2048:2176]

    def _stats(self, tg, src, eps, with_mean, nfeat=1024.0):
        kb = self.kb
        PSF = self.PSF
        sl = slice(tg * TGS, (tg + 1) * TGS)
        srd = ("%s/%d" % (self._srcname, tg),)
        if with_mean:
            kb.mm([(PSF[:, 0, :], self.c_ones(), src[:, c, sl], c == 0, c == C - 1)
                   for c in range(C)], reads=srd + ("CST/ones",), writes=("PB0",))
        for c in range(C):
            kb.do("S", lambda s, c=c: s.activation(
                self.SQB[:, c % 2, :], src[:, c, sl], AF.Square),
                  reads=srd, writes=("SQB/%d" % (c % 2),))
            kb.mm([(PSF[:, 1, :], self.c_ones(), self.SQB[:, c % 2, :],
                    c == 0, c == C - 1)],
                  reads=("SQB/%d" % (c % 2), "CST/ones"), writes=("PB1",))
        inv = 1.0 / nfeat
        if with_mean:
            kb.do("V", lambda v: v.tensor_scalar_mul(self.STAm, PSF[:, 0, :], inv),
                  reads=("PB0",), writes=("PL/m",))
            kb.do("S", lambda s: s.activation(self.SC0, self.STAm, AF.Square),
                  reads=("PL/m",), writes=("PL/s0",))
            kb.do("V", lambda v: v.scalar_tensor_tensor(
                self.SC1, PSF[:, 1, :], inv, self.SC0, ALU.mult, ALU.subtract),
                  reads=("PB1", "PL/s0"), writes=("PL/s1",))
        else:
            kb.do("V", lambda v: v.tensor_scalar_mul(self.SC1, PSF[:, 1, :], inv),
                  reads=("PB1",), writes=("PL/s1",))
        kb.do("S", lambda s: s.activation(self.SC0, self.SC1, AF.Ln, bias=eps),
              reads=("PL/s1", "CF"), writes=("PL/s0",))
        kb.do("S", lambda s: s.activation(self.STAr, self.SC0, AF.Exp, scale=-0.5),
              reads=("PL/s0",), writes=("PL/r",))

    def tln(self, bi):
        kb = self.kb
        ST = self.ST
        XB = self.XB.rearrange("p (c t) -> p c t", t=T)
        x_ap = self.x_io.ap()
        self._srcname = "XB"
        for tg in range(NTG):
            sl = slice(tg * TGS, (tg + 1) * TGS)
            kb.dma(self.cX, "Q", ST[:, :, sl], x_ap[:, :, sl],
                   reads=("XIO",), writes=("ST/%d" % tg,))
            kb.do("V", lambda v, sl=sl: v.tensor_copy(XB[:, :, sl], ST[:, :, sl]),
                  reads=("ST/%d" % tg,), writes=("XB/%d" % tg,))
            self._stats(tg, XB, self.CF[:, 0:1], True)
            for c in range(C):
                s0 = "PL/s%d" % (c % 2)
                scv = self.SC0 if c % 2 == 0 else self.SC1
                kb.do("V", lambda v, c=c, scv=scv, sl=sl: v.tensor_sub(
                    scv, ST[:, c, sl], self.STAm),
                      reads=("ST/%d" % tg, "PL/m"), writes=(s0,))
                kb.do("V", lambda v, c=c, scv=scv, sl=sl: v.tensor_mul(
                    ST[:, c, sl], scv, self.STAr),
                      reads=(s0, "PL/r"), writes=("ST/%d" % tg,))
                kb.do("V", lambda v, c=c, sl=sl: v.tensor_copy(
                    XB[:, c, sl], ST[:, c, sl]),
                      reads=("ST/%d" % tg,), writes=("XB/%d" % tg,))

    def residual_out(self, scale):
        kb = self.kb
        ST = self.ST
        if scale != 1.0:
            for tg in range(NTG):
                sl = slice(tg * TGS, (tg + 1) * TGS)
                kb.do("V", lambda v, sl=sl: v.tensor_scalar_mul(
                    ST[:, :, sl], ST[:, :, sl], scale),
                      reads=(), writes=("ST/%d" % tg,))
        for tg in range(NTG):
            sl = slice(tg * TGS, (tg + 1) * TGS)
            kb.dma(self.cA, "G", self.x_io.ap()[:, :, sl], self.ST[:, :, sl],
                   reads=("ST/%d" % tg,), writes=("XIO",), accum=True)

    def b_l0ffn(self, bi):
        for l in range(2):
            self._l0ffn_layer(l)

    def _shift_cast(self):
        kb = self.kb
        ST = self.ST
        XB = self.XB.rearrange("p (c t) -> p c t", t=T)
        stwr = tuple("ST/%d" % t for t in range(NTG))
        xbwr = tuple("XB/%d" % t for t in range(NTG))
        kb.do("V", lambda v: v.tensor_copy(XB[:, 0:4, :], ST[:, 0:4, :]),
              reads=stwr, writes=xbwr)
        kb.do("V", lambda v: v.tensor_copy(XB[:, 4:8, 1:T], ST[:, 4:8, 0:T - 1]),
              reads=stwr, writes=xbwr)
        kb.do("V", lambda v: v.memset(XB[:, 4:8, 0:1], 0.0),
              reads=(), writes=("XB/0",))

    def _l0ffn_layer(self, l):
        kb = self.kb
        io = self.ios
        for b in ("YB", "ZB", "WAR"):
            kb.retag(b)
        ST = self.ST
        XB = self.XB.rearrange("p (c t) -> p c t", t=T)
        YB = self.YB.rearrange("p (c t) -> p c t", t=T)
        ZBv = self.ZB.rearrange("p (a b) -> p a b", b=512)
        PSF = self.PSF
        WQr = [self.WAR[:, 0:2048].rearrange("p (c f) -> p c f", f=256),
               self.WAR[:, 2048:4096].rearrange("p (c f) -> p c f", f=256)]
        WV = self.WAR[:, 4096:8192].rearrange("p (c f) -> p c f", f=512)
        WOv = self.WAR[:, 4096:8192].rearrange("p (c f) -> p c f", f=1024)

        for nm, dst, src in (
                ("SPR/qs", self.SPR[:, 0:2], io["p_l0_qs"].ap()[:, l, :]),
                ("SPR/ks", self.SPR[:, 2:4], io["p_l0_ks"].ap()[:, l, :]),
                ("SPR/lng", self.SPR[:, 8:16], io["p_l0_lng"].ap()[:, l, :]),
                ("SPR/lnb", self.SPR[:, 16:24], io["p_l0_lnb"].ap()[:, l, :]),
                ("SPR/b1a", self.SPR[:, 24:46], io["p_l0_b1a"].ap()[:, l, :]),
                ("SPR/b1g", self.SPR[:, 46:68], io["p_l0_b1g"].ap()[:, l, :]),
                ("SPR/b2", self.SPR[:, 68:76], io["p_l0_b2"].ap()[:, l, :])):
            kb.dma(self.cR, "Q", dst, src, writes=(nm,))

        self._shift_cast()

        mapq = self.MAPS
        wsrc = io["w_l0_qkv"].ap()
        for qk in range(2):
            kb.dma(self.cW, "G", mapq[:, 0, :],
                   io["r0aq" if qk == 0 else "r0ak"].ap(), writes=("MAPS/0",))
            kb.dma(self.cW, "G", mapq[:, 1, :],
                   io["r0bq" if qk == 0 else "r0bk"].ap(), writes=("MAPS/1",))
            scp = "SPR/qs" if qk == 0 else "SPR/ks"
            scap = self.SPR[:, 0:1] if qk == 0 else self.SPR[:, 2:3]
            scap_s = self.SPR[:, 1:2] if qk == 0 else self.SPR[:, 3:4]
            for oc in range(4):
                ring = oc % 2
                wv = WQr[ring]
                wreg = "WAR/R%d" % ring
                col = qk * 1024 + oc * 256
                kb.dma(self.cW, "G", wv[:], wsrc[l, :, :, col:col + 256],
                       writes=(wreg,))
                for tg in range(NTG):
                    sl = slice(tg * TGS, (tg + 1) * TGS)
                    b0 = (tg % 2)
                    b1 = 2 + (tg % 2)
                    b2 = 4 + (tg % 2)
                    kb.mm([(PSF[:, b0, :], wv[:, ic, 0:128], XB[:, ic, sl],
                            ic == 0, ic == C - 1) for ic in range(C)],
                          reads=(wreg, "XB/%d" % tg), writes=("PB%d" % b0,))
                    kb.mm([(PSF[:, b1, :], wv[:, ic, 128:256], XB[:, ic, sl],
                            ic == 0, ic == C - 1) for ic in range(C)],
                          reads=(wreg, "XB/%d" % tg), writes=("PB%d" % b1,))
                    sq = self.SQB[:, tg % 2, :]
                    sqr = "SQB/%d" % (tg % 2)
                    kb.do("S", lambda s, b0=b0, sq=sq: s.activation(
                        sq, PSF[:, b0, :], AF.Square),
                          reads=("PB%d" % b0,), writes=(sqr,))
                    kb.mm([(PSF[:, b2, :], self.c_o64(), sq, True, True)],
                          reads=(sqr, "CST/o64"), writes=("PB%d" % b2,))
                    kb.do("S", lambda s, b2=b2: s.activation(
                        self.SC0, PSF[:, b2, :], AF.Ln, bias=self.CF[:, 2:3]),
                          reads=("PB%d" % b2, "CF"), writes=("PL/s0",))
                    kb.do("S", lambda s: s.activation(
                        self.STAr, self.SC0, AF.Exp, scale=-0.5),
                          reads=("PL/s0",), writes=("PL/r",))
                    kb.do("V", lambda v, b0=b0, scap=scap: v.scalar_tensor_tensor(
                        self.SC0, PSF[:, b0, :], scap, self.STAr,
                        ALU.mult, ALU.mult),
                          reads=("PB%d" % b0, "PL/r", scp), writes=("PL/s0",))
                    kb.do("V", lambda v, b1=b1, scap_s=scap_s:
                          v.scalar_tensor_tensor(
                              self.SC1, PSF[:, b1, :], scap_s, self.STAr,
                              ALU.mult, ALU.mult),
                          reads=("PB%d" % b1, "PL/r", scp), writes=("PL/s1",))
                    kb.do("V", lambda v, sl=sl: v.tensor_mul(
                        self.SC0, self.SC0, mapq[:, 0, sl]),
                          reads=("PL/s0", "MAPS/0"), writes=("PL/s0",))
                    kb.do("V", lambda v, sl=sl: v.tensor_mul(
                        self.SC1, self.SC1, mapq[:, 1, sl]),
                          reads=("PL/s1", "MAPS/1"), writes=("PL/s1",))
                    yc = qk * 4 + oc
                    kb.do("V", lambda v, yc=yc, sl=sl: v.tensor_add(
                        YB[:, yc, sl], self.SC0, self.SC1),
                          reads=("PL/s0", "PL/s1"), writes=("YB/%d" % yc,))
        kb.dma(self.cW, "G", WV[:], wsrc[l, :, :, 2048:2560], writes=("WAR/C",))
        for tt in range(NTT):
            bk = 6 + tt % 2
            kb.mm([(PSF[:, bk, :], XB[:, ic, tt * 128:(tt + 1) * 128],
                    WV[:, ic, :], ic == 0, ic == C - 1) for ic in range(C)],
                  reads=("WAR/C", "XB/%d" % (tt // 4)), writes=("PB%d" % bk,))
            kb.do("V", lambda v, tt=tt, bk=bk: v.tensor_copy(
                ZBv[:, tt, :], PSF[:, bk, :]),
                  reads=("PB%d" % bk,), writes=("ZB/%d" % tt,))

        kb.dma(self.cW, "G", WOv[:, 0:4, :], io["w_l0_wo"].ap()[l],
               writes=("WAR/C",))

        ctx = XB
        for h in range(8):
            qc = h // 2
            qpart = slice((h % 2) * 64, (h % 2) * 64 + 64)
            for qt in range(NTT):
                self._attn_tile(
                    qsrc=YB[qpart, qc, qt * 128:(qt + 1) * 128],
                    qreads=("YB/%d" % qc,),
                    ksrc=lambda c0, n, qpart=qpart, qc=qc:
                        YB[qpart, 4 + qc, c0:c0 + n],
                    kreads=("YB/%d" % (4 + qc),),
                    vsrc=lambda kt, h=h: ZBv[:, kt, h * 64:(h + 1) * 64],
                    vreads=lambda kt: ("ZB/%d" % kt,),
                    qt=qt, local=True,
                    ctx_out=ctx[(h % 2) * 64:(h % 2) * 64 + 64, h // 2,
                                qt * 128:(qt + 1) * 128],
                    ctx_wr=("XB/%d" % (qt // 4),))

        for oc in range(C):
            for tg in range(NTG):
                sl = slice(tg * TGS, (tg + 1) * TGS)
                bk = (oc * NTG + tg) % 4
                kb.mm([(PSF[:, bk, :], WOv[:, ic, oc * 128:(oc + 1) * 128],
                        ctx[:, ic, sl], ic == 0, ic == 3) for ic in range(4)],
                      reads=("WAR/C", "XB/%d" % tg), writes=("PB%d" % bk,))
                kb.do("V", lambda v, oc=oc, sl=sl, bk=bk: v.tensor_add(
                    ST[:, oc, sl], ST[:, oc, sl], PSF[:, bk, :]),
                      reads=("PB%d" % bk,), writes=("ST/%d" % tg,))

        self._shift_cast()
        self._srcname = "XB"
        for tg in range(NTG):
            sl = slice(tg * TGS, (tg + 1) * TGS)
            self._stats(tg, XB, self.CF[:, 0:1], True)
            for c in range(C):
                s0 = "PL/s%d" % (c % 2)
                scv = self.SC0 if c % 2 == 0 else self.SC1
                kb.do("V", lambda v, c=c, scv=scv, sl=sl: v.tensor_sub(
                    scv, XB[:, c, sl], self.STAm),
                      reads=("XB/%d" % tg, "PL/m"), writes=(s0,))
                kb.do("V", lambda v, c=c, scv=scv: v.scalar_tensor_tensor(
                    scv, scv, self.SPR[:, 8 + c:9 + c], self.STAr,
                    ALU.mult, ALU.mult),
                      reads=(s0, "SPR/lng", "PL/r"), writes=(s0,))
                kb.do("V", lambda v, c=c, scv=scv, sl=sl: v.tensor_scalar_add(
                    XB[:, c, sl], scv, self.SPR[:, 16 + c:17 + c]),
                      reads=(s0, "SPR/lnb"), writes=("XB/%d" % tg,))

        kb.retag("YB")
        w1a_sl = [self.WAR[:, 0:1024].rearrange("p (c f) -> p c f", f=128),
                  self.WAR[:, 1024:2048].rearrange("p (c f) -> p c f", f=128)]
        w1g_sl = [self.WAR[:, 2048:3072].rearrange("p (c f) -> p c f", f=128),
                  self.WAR[:, 3072:4096].rearrange("p (c f) -> p c f", f=128)]
        w2e = self.WAR[:, 4096:6912].rearrange("p (c f) -> p c f", f=128)
        YBg = self.YB.rearrange("p (a b) -> p a b", b=512)
        for tg in range(NTG):
            sl = slice(tg * TGS, (tg + 1) * TGS)
            for j in range(22):
                m = 128 if j < 21 else 42
                ring = j % 2
                wa = w1a_sl[ring]
                wg = w1g_sl[ring]
                kb.dma(self.cW, "G", wa[:, :, 0:m],
                       io["w_l0_w1a"].ap()[l, :, :, j * 128:j * 128 + m],
                       writes=("WAR/R0/%d" % ring,))
                kb.dma(self.cW, "G", wg[:, :, 0:m],
                       io["w_l0_w1g"].ap()[l, :, :, j * 128:j * 128 + m],
                       writes=("WAR/R1/%d" % ring,))
                ba = (j % 2) * 2
                bg = ba + 1
                kb.mm([(PSF[0:m, ba, :], wa[:, ic, 0:m], XB[:, ic, sl],
                        ic == 0, ic == C - 1) for ic in range(C)],
                      reads=("WAR/R0/%d" % ring, "XB/%d" % tg),
                      writes=("PB%d" % ba,))
                kb.mm([(PSF[0:m, bg, :], wg[:, ic, 0:m], XB[:, ic, sl],
                        ic == 0, ic == C - 1) for ic in range(C)],
                      reads=("WAR/R1/%d" % ring, "XB/%d" % tg),
                      writes=("PB%d" % bg,))
                sq = self.SQB[:, j % 2, :]
                sqr = "SQB/%d" % (j % 2)
                kb.do("S", lambda s, m=m, bg=bg, j=j, sq=sq: s.activation(
                    sq[0:m, :], PSF[0:m, bg, :], AF.Gelu,
                    bias=self.SPR[0:m, 46 + j:47 + j]),
                      reads=("PB%d" % bg, "SPR/b1g"), writes=(sqr,))
                kb.do("V", lambda v, m=m, ba=ba, j=j, sq=sq:
                      v.scalar_tensor_tensor(
                          YBg[0:m, j, :], PSF[0:m, ba, :],
                          self.SPR[0:m, 24 + j:25 + j], sq[0:m, :],
                          ALU.add, ALU.mult),
                      reads=("PB%d" % ba, "SPR/b1a", sqr),
                      writes=("YB/g%d" % j,))
            for oco in range(8):
                kb.dma(self.cW, "G", w2e[:],
                       io["w_l0_w2"].ap()[l, :, :, oco * 128:(oco + 1) * 128],
                       writes=("WAR/C",))
                bk = 4 + (oco % 4)
                mms = []
                for j in range(22):
                    m = 128 if j < 21 else 42
                    mms.append((PSF[:, bk, :], w2e[0:m, j, :],
                                YBg[0:m, j, :], j == 0, j == 21))
                kb.mm(mms, reads=("WAR/C",) + tuple("YB/g%d" % j
                                                    for j in range(22)),
                      writes=("PB%d" % bk,))
                kb.do("V", lambda v, oco=oco, sl=sl, bk=bk:
                      v.scalar_tensor_tensor(
                          ST[:, oco, sl], PSF[:, bk, :],
                          self.SPR[:, 68 + oco:69 + oco], ST[:, oco, sl],
                          ALU.add, ALU.add),
                      reads=("PB%d" % bk, "SPR/b2"), writes=("ST/%d" % tg,))

    def _attn_tile(self, qsrc, qreads, ksrc, kreads, vsrc, vreads, qt, local,
                   ctx_out, ctx_wr, scale=1.0):
        kb = self.kb
        PSF = self.PSF
        if local:
            k0 = max(0, qt * 128 - 512)
        else:
            k0 = 0
        k1 = qt * 128 + 128
        cols = k1 - k0
        nk = (cols + 511) // 512
        for g in range(nk):
            c0 = k0 + g * 512
            n = min(512, k1 - c0)
            kb.mm([(PSF[:, g, 0:n], qsrc, ksrc(c0, n), True, True)],
                  reads=qreads + kreads, writes=("PB%d" % g,))
        pbs = tuple("PB%d" % g for g in range(nk))
        psflat = PSF[:].rearrange("p a b -> p (a b)")
        if local and qt >= 4:
            kb.do("V", lambda v, cols=cols: v.tensor_add(
                self.AUS[:, 0:cols], psflat[:, 0:cols],
                self.c_maskl()[:, 0:cols]),
                  reads=pbs + ("CST/maskl",), writes=("PL",))
        else:
            if cols > 128:
                kb.do("V", lambda v, cols=cols: v.tensor_copy(
                    self.AUS[:, 0:cols - 128], psflat[:, 0:cols - 128]),
                      reads=pbs, writes=("PL",))
            kb.do("V", lambda v, cols=cols: v.tensor_add(
                self.AUS[:, cols - 128:cols], psflat[:, cols - 128:cols],
                self.c_maskd()),
                  reads=pbs + ("CST/maskd",), writes=("PL",))
        nmx = self.SML[:, 0:1]
        zz = self.SML[:, 1:2]
        rz = self.SML[:, 2:3]
        kb.do("V", lambda v, cols=cols: v.reduce_max(
            nmx, self.AUS[:, 0:cols], axis=AX.X, negate=True),
              reads=("PL",), writes=("SML/nm",))
        if scale != 1.0:
            nm2 = self.SML[:, 3:4]
            kb.do("V", lambda v: v.tensor_scalar_mul(nm2, nmx, scale),
                  reads=("SML/nm",), writes=("SML/nm2",))
            nmu, nmur = nm2, "SML/nm2"
        else:
            nmu, nmur = nmx, "SML/nm"
        kb.do("S", lambda s, cols=cols, nmu=nmu: s.activation(
            self.AUS[:, 0:cols], self.AUS[:, 0:cols], AF.Exp,
            bias=nmu, scale=scale, accum_out=zz),
              reads=("PL", nmur), writes=("PL", "SML/z"))
        kb.do("V", lambda v: v.reciprocal(rz, zz),
              reads=("SML/z",), writes=("SML/rz",))
        kb.do("V", lambda v, cols=cols: v.tensor_scalar_mul(
            self.AUS[:, 0:cols], self.AUS[:, 0:cols], rz),
              reads=("PL", "SML/rz"), writes=("PL",))
        nkt = (cols + 127) // 128
        for j0 in range(0, nkt, 4):
            jn = min(4, nkt - j0)
            bk = 2 + (j0 // 4) % 2
            kb.mm([(PSF[:, bk, jj * 128:(jj + 1) * 128],
                    self.AUS[:, (j0 + jj) * 128:(j0 + jj + 1) * 128], self.CID[:],
                    jj == 0, jj == jn - 1, dict(is_transpose=True))
                   for jj in range(jn)],
                  reads=("PL", "CID"), writes=("PB%d" % bk,))
            kb.do("V", lambda v, bk=bk, j0=j0, jn=jn: v.tensor_copy(
                self.ATR[:, j0 * 128:(j0 + jn) * 128], PSF[:, bk, 0:jn * 128]),
                  reads=("PB%d" % bk,), writes=("ATR",))
        bk = 4 + qt % 2
        vr = []
        for j in range(nkt):
            vr.extend(vreads(k0 // 128 + j))
        kb.mm([(PSF[0:64, bk, 0:128], vsrc(k0 // 128 + j),
                self.ATR[:, j * 128:(j + 1) * 128],
                j == 0, j == nkt - 1) for j in range(nkt)],
              reads=("ATR",) + tuple(vr), writes=("PB%d" % bk,))
        kb.do("V", lambda v, bk=bk: v.tensor_copy(ctx_out, PSF[0:64, bk, 0:128]),
              reads=("PB%d" % bk,), writes=ctx_wr)

    def b_l0attn(self, bi):
        kb = self.kb
        io = self.ios
        for b in ("YB", "ZB", "WAR"):
            kb.retag(b)
        ST = self.ST
        XB = self.XB.rearrange("p (c t) -> p c t", t=T)
        YB = self.YB.rearrange("p (c t) -> p c t", t=T)
        ZBv = self.ZB.rearrange("p (a b) -> p a b", b=1024)
        PSF = self.PSF
        WARv = self.WAR.rearrange("p (c f) -> p c f", f=1024)

        kb.dma(self.cR, "Q", self.SPR[:, 0:8], io["p_l0a_bo"].ap(),
               writes=("SPR/bo",))
        kb.dma(self.cW, "G", WARv[:], io["w_l0a_q"].ap(), writes=("WAR",))

        for oc in range(C):
            for tg in range(NTG):
                sl = slice(tg * TGS, (tg + 1) * TGS)
                bk = (oc * NTG + tg) % 4
                kb.mm([(PSF[:, bk, :], WARv[:, ic, oc * 128:(oc + 1) * 128],
                        XB[:, ic, sl], ic == 0, ic == C - 1) for ic in range(C)],
                      reads=("WAR", "XB/%d" % tg), writes=("PB%d" % bk,))
                kb.do("V", lambda v, oc=oc, sl=sl, bk=bk: v.tensor_copy(
                    YB[:, oc, sl], PSF[:, bk, :]),
                      reads=("PB%d" % bk,), writes=("YB/%d" % oc,))
        for tt in range(NTT):
            for half in range(2):
                bk = 4 + (tt * 2 + half) % 4
                kb.mm([(PSF[:, bk, :], XB[:, ic, tt * 128:(tt + 1) * 128],
                        WARv[:, ic, half * 512:(half + 1) * 512],
                        ic == 0, ic == C - 1) for ic in range(C)],
                      reads=("WAR", "XB/%d" % (tt // 4)), writes=("PB%d" % bk,))
                kb.do("V", lambda v, tt=tt, half=half, bk=bk: v.tensor_copy(
                    ZBv[:, tt, half * 512:(half + 1) * 512], PSF[:, bk, :]),
                      reads=("PB%d" % bk,), writes=("ZB/%d" % tt,))

        kb.dma(self.cW, "G", WARv[:], io["w_l0a_o"].ap(), writes=("WAR",))

        ctx = XB
        for h in range(16):
            qc = h // 2
            qpart = slice((h % 2) * 64, (h % 2) * 64 + 64)
            for qt in range(NTT):
                self._attn_tile(
                    qsrc=YB[qpart, qc, qt * 128:(qt + 1) * 128],
                    qreads=("YB/%d" % qc,),
                    ksrc=lambda c0, n, qpart=qpart, qc=qc:
                        YB[qpart, qc, c0:c0 + n],
                    kreads=("YB/%d" % qc,),
                    vsrc=lambda kt, h=h: ZBv[:, kt, h * 64:(h + 1) * 64],
                    vreads=lambda kt: ("ZB/%d" % kt,),
                    qt=qt, local=False, scale=0.125,
                    ctx_out=ctx[(h % 2) * 64:(h % 2) * 64 + 64, h // 2,
                                qt * 128:(qt + 1) * 128],
                    ctx_wr=("XB/%d" % (qt // 4),))

        for oc in range(C):
            for tg in range(NTG):
                sl = slice(tg * TGS, (tg + 1) * TGS)
                bk = (oc * NTG + tg) % 4
                kb.mm([(PSF[:, bk, :], WARv[:, ic, oc * 128:(oc + 1) * 128],
                        ctx[:, ic, sl], ic == 0, ic == C - 1) for ic in range(C)],
                      reads=("WAR", "XB/%d" % tg), writes=("PB%d" % bk,))
                kb.do("S", lambda s, oc=oc, sl=sl, bk=bk: s.add(
                    ST[:, oc, sl], PSF[:, bk, :], self.SPR[:, oc:oc + 1]),
                      reads=("PB%d" % bk, "SPR/bo"), writes=("ST/%d" % tg,))

    def b_l1ffn(self, bi):
        kb = self.kb
        io = self.ios
        for b in ("YB", "ZB", "WAR"):
            kb.retag(b)
        ST = self.ST
        XB = self.XB.rearrange("p (c t) -> p c t", t=T)
        YB = self.YB.rearrange("p (c t) -> p c t", t=T)
        ZBv = self.ZB.rearrange("p (a b) -> p a b", b=1024)
        PSF = self.PSF
        WARv = self.WAR.rearrange("p (c f) -> p c f", f=1024)
        QKH = self.WAR[:, 4096:8192].rearrange("p (a t) -> p a t", t=T)
        WQr = [self.WAR[:, 0:2048].rearrange("p (c f) -> p c f", f=256),
               self.WAR[:, 2048:4096].rearrange("p (c f) -> p c f", f=256)]

        kb.dma(self.cR, "Q", self.SPR[:, 0:8], io["p_l1f_s1"].ap(),
               writes=("SPR/s1",))
        kb.dma(self.cR, "Q", self.SPR[:, 8:16], io["p_l1f_s2"].ap(),
               writes=("SPR/s2",))

        self._srcname = "XB"
        for tg in range(NTG):
            sl = slice(tg * TGS, (tg + 1) * TGS)
            self._stats(tg, XB, self.CF[:, 1:2], False)
            for c in range(C):
                kb.do("V", lambda v, c=c, sl=sl: v.scalar_tensor_tensor(
                    XB[:, c, sl], ST[:, c, sl], self.SPR[:, c:c + 1],
                    self.STAr, ALU.mult, ALU.mult),
                      reads=("ST/%d" % tg, "SPR/s1", "PL/r"),
                      writes=("XB/%d" % tg,))

        kb.dma(self.cW, "G", WARv[:], io["w_l1f_v"].ap(), writes=("WAR",))
        for tt in range(NTT):
            for half in range(2):
                bk = (tt * 2 + half) % 4
                kb.mm([(PSF[:, bk, :], XB[:, ic, tt * 128:(tt + 1) * 128],
                        WARv[:, ic, half * 512:(half + 1) * 512],
                        ic == 0, ic == C - 1) for ic in range(C)],
                      reads=("WAR", "XB/%d" % (tt // 4)), writes=("PB%d" % bk,))
                kb.do("V", lambda v, tt=tt, half=half, bk=bk: v.tensor_copy(
                    ZBv[:, tt, half * 512:(half + 1) * 512], PSF[:, bk, :]),
                      reads=("PB%d" % bk,), writes=("ZB/%d" % tt,))

        mapq = self.MAPS
        kb.dma(self.cW, "G", mapq[:, 0, :], io["r1ak"].ap(), writes=("MAPS/0",))
        kb.dma(self.cW, "G", mapq[:, 1, :], io["r1bk"].ap(), writes=("MAPS/1",))
        ctx = YB
        for hp in range(8):
            for qk in range(2):
                wsrc = io["w_l1f_q" if qk == 0 else "w_l1f_k"].ap()
                ring = qk
                wv = WQr[ring]
                wreg = "WAR/R%d" % ring
                kb.dma(self.cW, "G", wv[:], wsrc[:, :, hp * 256:hp * 256 + 256],
                       writes=(wreg,))
                for tg in range(NTG):
                    sl = slice(tg * TGS, (tg + 1) * TGS)
                    b0 = tg % 2
                    b1 = 2 + tg % 2
                    kb.mm([(PSF[:, b0, :], wv[:, ic, 0:128], XB[:, ic, sl],
                            ic == 0, ic == C - 1) for ic in range(C)],
                          reads=(wreg, "XB/%d" % tg), writes=("PB%d" % b0,))
                    kb.mm([(PSF[:, b1, :], wv[:, ic, 128:256], XB[:, ic, sl],
                            ic == 0, ic == C - 1) for ic in range(C)],
                          reads=(wreg, "XB/%d" % tg), writes=("PB%d" % b1,))
                    kb.do("V", lambda v, b0=b0, sl=sl: v.tensor_mul(
                        self.SC0, PSF[:, b0, :], mapq[:, 0, sl]),
                          reads=("PB%d" % b0, "MAPS/0"), writes=("PL/s0",))
                    kb.do("V", lambda v, b1=b1, sl=sl: v.tensor_mul(
                        self.SC1, PSF[:, b1, :], mapq[:, 1, sl]),
                          reads=("PB%d" % b1, "MAPS/1"), writes=("PL/s1",))
                    kb.do("V", lambda v, qk=qk, sl=sl: v.tensor_add(
                        QKH[:, qk, sl], self.SC0, self.SC1),
                          reads=("PL/s0", "PL/s1"), writes=("WAR/H%d" % qk,))
            for hh in range(2):
                h = hp * 2 + hh
                qpart = slice(hh * 64, hh * 64 + 64)
                for qt in range(NTT):
                    self._attn_tile(
                        qsrc=QKH[qpart, 0, qt * 128:(qt + 1) * 128],
                        qreads=("WAR/H0",),
                        ksrc=lambda c0, n, qpart=qpart: QKH[qpart, 1, c0:c0 + n],
                        kreads=("WAR/H1",),
                        vsrc=lambda kt, h=h: ZBv[:, kt, h * 64:(h + 1) * 64],
                        vreads=lambda kt: ("ZB/%d" % kt,),
                        qt=qt, local=False, scale=0.125,
                        ctx_out=ctx[hh * 64:hh * 64 + 64, hp,
                                    qt * 128:(qt + 1) * 128],
                        ctx_wr=("YB/%d" % hp,))

        kb.dma(self.cW, "G", WARv[:], io["w_l1f_o"].ap(), writes=("WAR",))
        for oc in range(C):
            for tg in range(NTG):
                sl = slice(tg * TGS, (tg + 1) * TGS)
                bk = (oc * NTG + tg) % 4
                kb.mm([(PSF[:, bk, :], WARv[:, ic, oc * 128:(oc + 1) * 128],
                        ctx[:, ic, sl], ic == 0, ic == C - 1) for ic in range(C)],
                      reads=("WAR",) + tuple("YB/%d" % i for i in range(8)),
                      writes=("PB%d" % bk,))
                kb.do("V", lambda v, oc=oc, sl=sl, bk=bk: v.tensor_add(
                    ST[:, oc, sl], ST[:, oc, sl], PSF[:, bk, :]),
                      reads=("PB%d" % bk,), writes=("ST/%d" % tg,))

        for tg in range(NTG):
            sl = slice(tg * TGS, (tg + 1) * TGS)
            kb.do("V", lambda v, sl=sl: v.tensor_copy(XB[:, :, sl], ST[:, :, sl]),
                  reads=("ST/%d" % tg,), writes=("XB/%d" % tg,))
            self._stats(tg, XB, self.CF[:, 1:2], False)
            for c in range(C):
                kb.do("V", lambda v, c=c, sl=sl: v.scalar_tensor_tensor(
                    XB[:, c, sl], ST[:, c, sl], self.SPR[:, 8 + c:9 + c],
                    self.STAr, ALU.mult, ALU.mult),
                      reads=("ST/%d" % tg, "SPR/s2", "PL/r"),
                      writes=("XB/%d" % tg,))

        kb.dma(self.cW, "G", self.WAR[:, 0:64].rearrange("p (c f) -> p c f", f=8),
               io["w_l1f_r"].ap(), writes=("WAR/R0",))
        wrt = self.WAR[:, 0:64].rearrange("p (c f) -> p c f", f=8)
        SM = self.SML
        for tt in range(NTT):
            bk = tt % 2
            kb.mm([(PSF[:, bk, 0:8], XB[:, ic, tt * 128:(tt + 1) * 128],
                    wrt[:, ic, :], ic == 0, ic == C - 1) for ic in range(C)],
                  reads=("WAR/R0", "XB/%d" % (tt // 4)), writes=("PB%d" % bk,))
            lg = SM[:, 8:16]
            nr = "SML/t%d" % (tt % 2)
            nsv = self.SPR[:, 144 + (tt % 2) * 8:152 + (tt % 2) * 8]
            kb.dma(self.cR, "Q", nsv, io["noise_st"].ap()[:, tt, :],
                   writes=("SPR/ns%d" % (tt % 2),))
            kb.do("V", lambda v, bk=bk, lg=lg, nsv=nsv: v.tensor_add(
                lg, PSF[:, bk, 0:8], nsv),
                  reads=("PB%d" % bk, "SPR/ns%d" % (tt % 2)), writes=(nr,))
            kb.do("V", lambda v, lg=lg: v.reduce_max(
                SM[:, 16:17], lg, axis=AX.X, negate=True),
                  reads=(nr,), writes=("SML/a",))
            kb.do("S", lambda s, lg=lg: s.activation(
                SM[:, 24:32], lg, AF.Exp, bias=SM[:, 16:17],
                accum_out=SM[:, 17:18]),
                  reads=(nr, "SML/a"), writes=("SML/e8", "SML/z8"))
            kb.do("V", lambda v: v.reciprocal(SM[:, 18:19], SM[:, 17:18]),
                  reads=("SML/z8",), writes=("SML/rz8",))
            kb.do("V", lambda v: v.tensor_scalar_mul(
                SM[:, 8:16], SM[:, 24:32], SM[:, 18:19]),
                  reads=("SML/e8", "SML/rz8"), writes=(nr,))
            kb.do("V", lambda v, lg=lg: v.reduce_max(SM[:, 19:20], lg, axis=AX.X),
                  reads=(nr,), writes=("SML/m1",))
            kb.do("S", lambda s, lg=lg: s.activation(
                SM[:, 32:40], lg, AF.Sign, bias=SM[:, 19:20], scale=-1.0),
                  reads=(nr, "SML/m1"), writes=("SML/u1",))
            kb.do("V", lambda v, lg=lg: v.tensor_mul(
                SM[:, 40:48], lg, SM[:, 32:40]),
                  reads=(nr, "SML/u1"), writes=("SML/p2",))
            kb.do("V", lambda v: v.reduce_max(SM[:, 20:21], SM[:, 40:48],
                                              axis=AX.X),
                  reads=("SML/p2",), writes=("SML/m2",))
            kb.do("S", lambda s, lg=lg: s.activation(
                SM[:, 32:40], lg, AF.Sign, bias=SM[:, 20:21], scale=-1.0),
                  reads=(nr, "SML/m2"), writes=("SML/u1",))
            kb.do("V", lambda v: v.tensor_scalar(
                SM[:, 48:56], SM[:, 32:40], -1.0, 1.0, ALU.mult, ALU.add),
                  reads=("SML/u1",), writes=("SML/f",))
            kb.do("V", lambda v: v.tensor_scalar_min(
                SM[:, 48:56], SM[:, 48:56], 1.0),
                  reads=("SML/f",), writes=("SML/f",))
            kb.do("V", lambda v, lg=lg: v.tensor_mul(
                SM[:, 56:64], lg, SM[:, 48:56]),
                  reads=(nr, "SML/f"), writes=("SML/g",))
            bt = 4 + tt % 2
            kb.mm([(PSF[0:8, bt, 0:128], SM[:, 56:64], self.CID[:],
                    True, True, dict(is_transpose=True))],
                  reads=("SML/g", "CID"), writes=("PB%d" % bt,))
            kb.do("V", lambda v, tt=tt, bt=bt: v.tensor_copy(
                self.GRW[:, tt * 128:(tt + 1) * 128], PSF[0:8, bt, 0:128]),
                  reads=("PB%d" % bt,), writes=("MAPS/g",))
        kb.dma(self.cW, "G", self.BOUT[:], io["w_l1f_bo"].ap(),
               writes=("MAPS/bo",))

        w1sl = [self.WAR[:, 0:1024].rearrange("p (c f) -> p c f", f=128),
                self.WAR[:, 1024:2048].rearrange("p (c f) -> p c f", f=128)]
        w2sl = [self.WAR[:, 2048:3072].rearrange("p (c f) -> p c f", f=128),
                self.WAR[:, 3072:4096].rearrange("p (c f) -> p c f", f=128)]
        wosl = [self.WAR[:, 4096:5120].rearrange("p (c f) -> p c f", f=128),
                self.WAR[:, 5120:6144].rearrange("p (c f) -> p c f", f=128)]
        kb.dma(self.cR, "Q", self.SPR[:, 16:80],
               io["p_l1f_b1"].ap().rearrange("p a b -> p (a b)"),
               writes=("SPR/be1",))
        kb.dma(self.cR, "Q", self.SPR[:, 80:144],
               io["p_l1f_b2"].ap().rearrange("p a b -> p (a b)"),
               writes=("SPR/be2",))
        kb.retag("YB")
        for tg in range(NTG):
            sl = slice(tg * TGS, (tg + 1) * TGS)
            for e in range(8):
                kb.mm([(PSF[:, 6, :], self.c_sel(e), self.GRW[:, sl],
                        True, True)],
                      reads=("CST/sel", "MAPS/g"), writes=("PB6",))
                for oc in range(C):
                    ring = oc % 2
                    b1k = ring
                    b2k = 2 + ring
                    kb.dma(self.cW, "G", w1sl[ring][:],
                           io["w_l1f_e1"].ap()[e, :, :, oc * 128:(oc + 1) * 128],
                           writes=("WAR/R0/%d" % ring,))
                    kb.dma(self.cW, "G", w2sl[ring][:],
                           io["w_l1f_e2"].ap()[e, :, :, oc * 128:(oc + 1) * 128],
                           writes=("WAR/R1/%d" % ring,))
                    kb.mm([(PSF[:, b1k, :], w1sl[ring][:, ic, :], XB[:, ic, sl],
                            ic == 0, ic == C - 1) for ic in range(C)],
                          reads=("WAR/R0/%d" % ring, "XB/%d" % tg),
                          writes=("PB%d" % b1k,))
                    kb.mm([(PSF[:, b2k, :], w2sl[ring][:, ic, :], XB[:, ic, sl],
                            ic == 0, ic == C - 1) for ic in range(C)],
                          reads=("WAR/R1/%d" % ring, "XB/%d" % tg),
                          writes=("PB%d" % b2k,))
                    sq = self.SQB[:, ring, :]
                    sqr = "SQB/%d" % ring
                    kb.do("S", lambda s, b1k=b1k, e=e, oc=oc, sq=sq:
                          s.activation(sq, PSF[:, b1k, :], AF.Silu,
                                       bias=self.SPR[:, 16 + e * 8 + oc:
                                                     17 + e * 8 + oc]),
                          reads=("PB%d" % b1k, "SPR/be1"), writes=(sqr,))
                    scv = self.SC0 if ring == 0 else self.SC1
                    spl = "PL/s%d" % ring
                    kb.do("V", lambda v, b2k=b2k, e=e, oc=oc, sq=sq, scv=scv:
                          v.scalar_tensor_tensor(
                              scv, PSF[:, b2k, :],
                              self.SPR[:, 80 + e * 8 + oc:81 + e * 8 + oc],
                              sq, ALU.add, ALU.mult),
                          reads=("PB%d" % b2k, "SPR/be2", sqr), writes=(spl,))
                    kb.do("V", lambda v, oc=oc, scv=scv: v.tensor_mul(
                        YB[:, oc, 0:TGS], scv, PSF[:, 6, :]),
                          reads=(spl, "PB6"), writes=("YB/%d" % oc,))
                for oc2 in range(C):
                    ring = oc2 % 2
                    bo = 4 + ring
                    kb.dma(self.cW, "G", wosl[ring][:],
                           io["w_l1f_eo"].ap()[e, :, :, oc2 * 128:(oc2 + 1) * 128],
                           writes=("WAR/R2/%d" % ring,))
                    mms = [(PSF[:, bo, :], wosl[ring][:, ic, :], YB[:, ic, 0:TGS],
                            ic == 0, False) for ic in range(C)]
                    if e == 0:
                        mms.append((PSF[:, bo, :],
                                    self.BOUT[:, oc2 * 128:(oc2 + 1) * 128],
                                    self.GRW[:, sl], False, True))
                    else:
                        last = mms.pop()
                        mms.append((last[0], last[1], last[2], last[3], True))
                    rd = ("WAR/R2/%d" % ring,) + tuple("YB/%d" % i
                                                      for i in range(C))
                    if e == 0:
                        rd = rd + ("MAPS/bo", "MAPS/g")
                    kb.mm(mms, reads=rd, writes=("PB%d" % bo,))
                    kb.do("V", lambda v, oc2=oc2, sl=sl, bo=bo: v.tensor_add(
                        ST[:, oc2, sl], ST[:, oc2, sl], PSF[:, bo, :]),
                          reads=("PB%d" % bo,), writes=("ST/%d" % tg,))

    def b_l1attn(self, bi):
        for l in range(6):
            self._l1attn_layer(l)

    def _ln_gb(self, tg, XBv, wof, bof, spw, spb, eps):
        kb = self.kb
        ST = self.ST
        sl = slice(tg * TGS, (tg + 1) * TGS)
        self._srcname = "XB"
        self._stats(tg, XBv, eps, True)
        for c in range(C):
            s0 = "PL/s%d" % (c % 2)
            scv = self.SC0 if c % 2 == 0 else self.SC1
            kb.do("V", lambda v, c=c, scv=scv, sl=sl: v.tensor_sub(
                scv, ST[:, c, sl], self.STAm),
                  reads=("ST/%d" % tg, "PL/m"), writes=(s0,))
            kb.do("V", lambda v, c=c, scv=scv: v.scalar_tensor_tensor(
                scv, scv, self.SPR[:, wof + c:wof + c + 1], self.STAr,
                ALU.mult, ALU.mult),
                  reads=(s0, spw, "PL/r"), writes=(s0,))
            kb.do("V", lambda v, c=c, scv=scv, sl=sl: v.tensor_scalar_add(
                ST[:, c, sl], scv, self.SPR[:, bof + c:bof + c + 1]),
                  reads=(s0, spb), writes=("ST/%d" % tg,))
            kb.do("V", lambda v, c=c, sl=sl: v.tensor_copy(
                XBv[:, c, sl], ST[:, c, sl]),
                  reads=("ST/%d" % tg,), writes=("XB/%d" % tg,))

    def _l1attn_layer(self, l):
        kb = self.kb
        io = self.ios
        for b in ("YB", "ZB", "WAR", "XB"):
            kb.retag(b)
        ST = self.ST
        XB = self.XB.rearrange("p (c t) -> p c t", t=T)
        XBt = self.XB.rearrange("p (a b) -> p a b", b=1024)
        YB = self.YB.rearrange("p (c t) -> p c t", t=T)
        ZB = self.ZB.rearrange("p (c t) -> p c t", t=T)
        PSF = self.PSF
        WQr = [self.WAR[:, 0:1024].rearrange("p (c f) -> p c f", f=128),
               self.WAR[:, 1024:2048].rearrange("p (c f) -> p c f", f=128)]
        WKr = [self.WAR[:, 2048:3072].rearrange("p (c f) -> p c f", f=128),
               self.WAR[:, 3072:4096].rearrange("p (c f) -> p c f", f=128)]

        for nm, cols, key in (("bq", (0, 8), "p_l1a_bq"), ("bk", (8, 16), "p_l1a_bk"),
                              ("l1w", (16, 24), "p_l1a_l1w"),
                              ("l1b", (24, 32), "p_l1a_l1b"),
                              ("l2w", (32, 40), "p_l1a_l2w"),
                              ("l2b", (40, 48), "p_l1a_l2b"),
                              ("l3w", (48, 56), "p_l1a_l3w"),
                              ("l3b", (56, 64), "p_l1a_l3b"),
                              ("b1", (64, 96), "p_l1a_b1"),
                              ("b2", (96, 104), "p_l1a_b2")):
            kb.dma(self.cR, "Q", self.SPR[:, cols[0]:cols[1]],
                   io[key].ap()[:, l, :], writes=("SPR/" + nm,))

        for tg in range(NTG):
            sl = slice(tg * TGS, (tg + 1) * TGS)
            kb.do("V", lambda v, sl=sl: v.tensor_copy(XB[:, :, sl], ST[:, :, sl]),
                  reads=("ST/%d" % tg,), writes=("XB/%d" % tg,))
        for hp in range(8):
            for qk in range(2):
                wr = (WQr if qk == 0 else WKr)[hp % 2]
                wreg = "WAR/%s%d" % ("Q" if qk == 0 else "K", hp % 2)
                kb.dma(self.cW, "G", wr[:],
                       io["w_l1a_q" if qk == 0 else "w_l1a_k"].ap()
                       [l, :, :, hp * 128:(hp + 1) * 128],
                       writes=(wreg,))
                dst = YB if qk == 0 else ZB
                dreg = ("YB/%d" if qk == 0 else "ZB/%d") % hp
                boff = 0 if qk == 0 else 8
                bsp = "SPR/bq" if qk == 0 else "SPR/bk"
                for tg in range(NTG):
                    sl = slice(tg * TGS, (tg + 1) * TGS)
                    bk = (hp * 2 + qk + tg) % 4
                    kb.mm([(PSF[:, bk, :], wr[:, ic, :], XB[:, ic, sl],
                            ic == 0, ic == C - 1) for ic in range(C)],
                          reads=(wreg, "XB/%d" % tg), writes=("PB%d" % bk,))
                    kb.do("S", lambda s, dst=dst, hp=hp, sl=sl, bk=bk, boff=boff:
                          s.add(dst[:, hp, sl], PSF[:, bk, :],
                                self.SPR[:, boff + hp:boff + hp + 1]),
                          reads=("PB%d" % bk, bsp), writes=(dreg,))

        xbwr = tuple("XB/%d" % t for t in range(NTG))
        for kt in range(NTT):
            for ec2 in range(2):
                bk = 4 + ec2
                kb.mm([(PSF[:, bk, jj * 128:(jj + 1) * 128],
                        ST[:, ec2 * 4 + jj, kt * 128:(kt + 1) * 128], self.CID[:],
                        jj == 0, jj == 3, dict(is_transpose=True))
                       for jj in range(4)],
                      reads=tuple("ST/%d" % t for t in range(NTG)) + ("CID",),
                      writes=("PB%d" % bk,))
                kb.do("V", lambda v, kt=kt, ec2=ec2, bk=bk: v.tensor_copy(
                    XBt[:, kt, ec2 * 512:(ec2 + 1) * 512], PSF[:, bk, 0:512]),
                      reads=("PB%d" % bk,), writes=xbwr)

        for qt in range(NTT):
            rmap = self.MAPS[:, qt % 2, :]
            rreg = "MAPS/%d" % (qt % 2)
            kb.dma(self.cW, "G", rmap, io["rel_st"].ap()[qt], writes=(rreg,))
            for h in range(16):
                qpart = slice((h % 2) * 64, (h % 2) * 64 + 64)
                qc = h // 2
                for kg in range(4):
                    kb.mm([(PSF[:, 4 + kg, :],
                            YB[qpart, qc, qt * 128:(qt + 1) * 128],
                            ZB[qpart, qc, kg * 512:(kg + 1) * 512], True, True)],
                          reads=("YB/%d" % qc, "ZB/%d" % qc),
                          writes=("PB%d" % (4 + kg),))
                pbs = ("PB4", "PB5", "PB6", "PB7")
                ps2 = PSF[:, 4:8, :].rearrange("p a b -> p (a b)")
                nmx = self.SML[:, 0:1]
                zz = self.SML[:, 1:2]
                rz = self.SML[:, 2:3]
                nm2 = self.SML[:, 3:4]
                kb.do("V", lambda v, ps2=ps2: v.reduce_max(
                    nmx, ps2, axis=AX.X, negate=True),
                      reads=pbs, writes=("SML/nm",))
                kb.do("V", lambda v: v.tensor_scalar_mul(nm2, nmx, 0.125),
                      reads=("SML/nm",), writes=("SML/nm2",))
                kb.do("S", lambda s, ps2=ps2: s.activation(
                    self.AUS[:], ps2, AF.Exp, bias=nm2, scale=0.125,
                    accum_out=zz),
                      reads=pbs + ("SML/nm2",), writes=("PL", "SML/z"))
                kb.do("V", lambda v: v.reciprocal(rz, zz),
                      reads=("SML/z",), writes=("SML/rz",))
                kb.do("S", lambda s: s.mul(self.ATR[:], self.AUS[:], rz),
                      reads=("PL", "SML/rz"), writes=("ATR",))
                kb.mm([(PSF[:, kg, :], self.c_id16(),
                        self.ATR[:, kg * 512:(kg + 1) * 512],
                        h == 0, h == 15) for kg in range(4)],
                      reads=("ATR", "CST/id16"),
                      writes=("PB0", "PB1", "PB2", "PB3"))
            wv = PSF[:, 0:4, :].rearrange("p a b -> p (a b)")
            kb.do("V", lambda v, wv=wv, rmap=rmap: v.tensor_add(
                self.AUS[:], wv, rmap),
                  reads=("PB0", "PB1", "PB2", "PB3", rreg), writes=("PL",))
            nmx = self.SML[:, 0:1]
            zz = self.SML[:, 1:2]
            rz = self.SML[:, 2:3]
            kb.do("V", lambda v: v.reduce_max(nmx, self.AUS[:], axis=AX.X,
                                              negate=True),
                  reads=("PL",), writes=("SML/nm",))
            kb.do("S", lambda s: s.activation(self.AUS[:], self.AUS[:], AF.Exp,
                                              bias=nmx, accum_out=zz),
                  reads=("PL", "SML/nm"), writes=("PL", "SML/z"))
            kb.do("V", lambda v: v.reciprocal(rz, zz),
                  reads=("SML/z",), writes=("SML/rz",))
            kb.do("V", lambda v: v.tensor_scalar_mul(self.AUS[:], self.AUS[:], rz),
                  reads=("PL", "SML/rz"), writes=("PL",))
            for j0 in range(0, 16, 4):
                bk = (j0 // 4) % 2
                kb.mm([(PSF[:, bk, jj * 128:(jj + 1) * 128],
                        self.AUS[:, (j0 + jj) * 128:(j0 + jj + 1) * 128],
                        self.CID[:], jj == 0, jj == 3, dict(is_transpose=True))
                       for jj in range(4)],
                      reads=("PL", "CID"), writes=("PB%d" % bk,))
                kb.do("V", lambda v, bk=bk, j0=j0: v.tensor_copy(
                    self.ATR[:, j0 * 128:(j0 + 4) * 128], PSF[:, bk, 0:512]),
                      reads=("PB%d" % bk,), writes=("ATR",))
            for ec in range(C):
                bk = 6 + ec % 2
                kb.mm([(PSF[:, bk, 0:128], XBt[:, kt, ec * 128:(ec + 1) * 128],
                        self.ATR[:, kt * 128:(kt + 1) * 128],
                        kt == 0, kt == NTT - 1) for kt in range(NTT)],
                      reads=("ATR",) + xbwr, writes=("PB%d" % bk,))
                kb.do("V", lambda v, ec=ec, qt=qt, bk=bk: v.tensor_add(
                    ST[:, ec, qt * 128:(qt + 1) * 128],
                    ST[:, ec, qt * 128:(qt + 1) * 128], PSF[:, bk, 0:128]),
                      reads=("PB%d" % bk,), writes=("ST/%d" % (qt // 4),))

        for tg in range(NTG):
            sl = slice(tg * TGS, (tg + 1) * TGS)
            kb.do("V", lambda v, sl=sl: v.tensor_copy(XB[:, :, sl], ST[:, :, sl]),
                  reads=("ST/%d" % tg,), writes=("XB/%d" % tg,))
            self._ln_gb(tg, XB, 16, 24, "SPR/l1w", "SPR/l1b", self.CF[:, 0:1])
            self._ln_gb(tg, XB, 32, 40, "SPR/l2w", "SPR/l2b", self.CF[:, 0:1])

        w1r = [self.WAR[:, 0:1024].rearrange("p (c f) -> p c f", f=128),
               self.WAR[:, 1024:2048].rearrange("p (c f) -> p c f", f=128)]
        w2ha = self.WAR[:, 2048:4096].rearrange("p (c f) -> p c f", f=128)
        w2hb = self.WAR[:, 4096:6144].rearrange("p (c f) -> p c f", f=128)
        YBh = self.YB.rearrange("p (a b) -> p a b", b=1024)
        ZBh = self.ZB.rearrange("p (a b) -> p a b", b=1024)
        kb.retag("YB")
        kb.retag("ZB")
        for th in range(2):
            hsl = slice(th * 1024, (th + 1) * 1024)
            for fc in range(32):
                ring = fc % 2
                kb.dma(self.cW, "G", w1r[ring][:],
                       io["w_l1a_1"].ap()[l, :, :, fc * 128:(fc + 1) * 128],
                       writes=("WAR/F%d" % ring,))
                hv = YBh if fc < 16 else ZBh
                pfx = "YB" if fc < 16 else "ZB"
                hreg = "%s/%d/%d" % (pfx, (fc % 16) // 2, fc % 2)
                for g2 in range(2):
                    sl = slice(th * 1024 + g2 * 512, th * 1024 + (g2 + 1) * 512)
                    bk = (fc + g2) % 4
                    kb.mm([(PSF[:, bk, :], w1r[ring][:, ic, :], XB[:, ic, sl],
                            ic == 0, ic == C - 1) for ic in range(C)],
                          reads=("WAR/F%d" % ring,
                                 "XB/%d" % (th * 2 + g2)),
                          writes=("PB%d" % bk,))
                    kb.do("S", lambda s, hv=hv, fc=fc, g2=g2, bk=bk:
                          s.activation(hv[:, fc % 16, g2 * 512:(g2 + 1) * 512],
                                       PSF[:, bk, :], AF.Relu,
                                       bias=self.SPR[:, 64 + fc:65 + fc]),
                          reads=("PB%d" % bk, "SPR/b1"), writes=(hreg,))
            hregs = tuple("YB/%d" % i for i in range(8)) + \
                tuple("ZB/%d" % i for i in range(8))
            for oc2 in range(C):
                kb.dma(self.cW, "G", w2ha[:],
                       io["w_l1a_2"].ap()[l, :, 0:16, oc2 * 128:(oc2 + 1) * 128],
                       writes=("WAR/GA",))
                kb.dma(self.cW, "G", w2hb[:],
                       io["w_l1a_2"].ap()[l, :, 16:32, oc2 * 128:(oc2 + 1) * 128],
                       writes=("WAR/GB",))
                for g2 in range(2):
                    sl = slice(th * 1024 + g2 * 512, th * 1024 + (g2 + 1) * 512)
                    bk = 4 + (oc2 * 2 + g2) % 4
                    mms = []
                    for fc in range(32):
                        hv = YBh if fc < 16 else ZBh
                        wsl = w2ha if fc < 16 else w2hb
                        mms.append((PSF[:, bk, :], wsl[:, fc % 16, :],
                                    hv[:, fc % 16, g2 * 512:(g2 + 1) * 512],
                                    fc == 0, fc == 31))
                    kb.mm(mms, reads=("WAR/GA", "WAR/GB") + hregs,
                          writes=("PB%d" % bk,))
                    kb.do("V", lambda v, oc2=oc2, sl=sl, bk=bk:
                          v.scalar_tensor_tensor(
                              ST[:, oc2, sl], PSF[:, bk, :],
                              self.SPR[:, 96 + oc2:97 + oc2], ST[:, oc2, sl],
                              ALU.add, ALU.add),
                          reads=("PB%d" % bk, "SPR/b2"),
                          writes=("ST/%d" % (th * 2 + g2),))

        for tg in range(NTG):
            sl = slice(tg * TGS, (tg + 1) * TGS)
            kb.do("V", lambda v, sl=sl: v.tensor_copy(XB[:, :, sl], ST[:, :, sl]),
                  reads=("ST/%d" % tg,), writes=("XB/%d" % tg,))
            self._ln_gb(tg, XB, 48, 56, "SPR/l3w", "SPR/l3b", self.CF[:, 0:1])


def build_nc(shapes, nblocks):
    nc = bass.Bass("TRN2", target_bir_lowering=False)
    p = Prog(nc, nblocks)
    p.build(shapes)
    return nc


def kernel(_nblocks=6, _cores=None, **inputs):
    sh, percore = _stage(inputs)
    cores = list(range(NCORES)) if _cores is None else _cores
    shapes = dict(sh)
    shapes.update(percore[0])
    nc = build_nc(shapes, _nblocks)
    in_maps = []
    for ci in cores:
        m = dict(sh)
        m.update(percore[ci])
        in_maps.append(m)
    res = run_bass_kernel_spmd(nc, in_maps, core_ids=list(range(len(cores))))
    out = np.empty((NCORES, T, D), np.float32)
    for i, ci in enumerate(cores):
        xo = res.results[i]["x_out"]
        out[ci] = np.asarray(xo, np.float32).transpose(2, 1, 0).reshape(T, D)
    return out


# version 03 (final)
# speedup vs baseline: 1.1782x; 1.1782x over previous
import contextlib

import numpy as np
import ml_dtypes
import concourse.bass as bass
import concourse.mybir as mybir
from concourse.bass_utils import run_bass_kernel_spmd

F32 = mybir.dt.float32
BF16 = mybir.dt.bfloat16
AF = mybir.ActivationFunctionType
ALU = mybir.AluOpType
AX = mybir.AxisListType
BF = ml_dtypes.bfloat16

NCORES = 8
T = 2048
D = 1024
C = 8
TGS = 512
NTG = T // TGS
NTT = T // 128
NEG = -3.0e38

ENGS = ("T", "V", "S", "G", "Q")


class KB:
    def __init__(self, nc, n_dma_channels=8):
        self.nc = nc
        self.stack = contextlib.ExitStack()
        self.prog = {e: [] for e in ENGS}
        self.cnt = {e: 0 for e in ENGS}
        self.sem = {}
        for e in ENGS:
            self.sem[e] = self.stack.enter_context(nc.semaphore("sem_" + e))
        self.dma_cnt = {}
        for i in range(n_dma_channels):
            ch = "D%d" % i
            self.sem[ch] = self.stack.enter_context(nc.semaphore("sem_" + ch))
            self.dma_cnt[ch] = 0
        self._chfree = ["D%d" % i for i in range(n_dma_channels)]
        self.hwm = {}
        self.regions = {}

    def channel(self):
        return self._chfree.pop(0)

    def _emit_wait(self, e, p, v):
        if p in self.dma_cnt:
            v = self.dma_cnt[p]
        if self.hwm.get((e, p), 0) >= v:
            return
        self.hwm[(e, p)] = v
        sem = self.sem[p]
        self.prog[e].append(lambda eng, sem=sem, v=v: eng.wait_ge(sem, v))

    def wait(self, e, *tokens):
        best = {}
        for tok in tokens:
            if tok is None:
                continue
            p, v = tok
            if v > best.get(p, 0):
                best[p] = v
        for p, v in best.items():
            self._emit_wait(e, p, v)

    def _raw_op(self, e, fn, inc):
        if inc:
            self.cnt[e] += 1
            v = self.cnt[e]
            sem = self.sem[e]
            self.prog[e].append(lambda eng, fn=fn, sem=sem: fn(eng).then_inc(sem, 1))
            return (e, v)
        self.prog[e].append(fn)
        return None

    def _rel(self, name):
        out = []
        for r in self.regions:
            if r == name or r.startswith(name + "/") or name.startswith(r + "/"):
                out.append(r)
        return out

    def _read_tokens(self, names):
        toks = []
        for n in names:
            for r in self._rel(n):
                toks.append(self.regions[r][0])
        return toks

    def _write_tokens(self, names):
        toks = []
        for n in names:
            for r in self._rel(n):
                w, rd = self.regions[r]
                toks.append(w)
                toks.extend(rd)
        return toks

    def _note_read(self, names, tok):
        for n in names:
            self.regions.setdefault(n, (None, []))[1].append(tok)

    def _note_write(self, names, tok):
        for n in names:
            for r in self._rel(n):
                if r != n:
                    del self.regions[r]
            self.regions[n] = (tok, [])

    def do(self, e, fn, reads=(), writes=(), extra=(), inc=True):
        toks = self._read_tokens(reads) + self._write_tokens(writes) + list(extra)
        self.wait(e, *toks)
        tok = self._raw_op(e, fn, inc)
        if tok is not None:
            self._note_read(reads, tok)
            self._note_write(writes, tok)
        return tok

    def dma(self, ch, qe, out_ap, in_ap, reads=(), writes=(), extra=(), accum=False):
        toks = self._read_tokens(reads) + self._write_tokens(writes) + list(extra)
        self.wait(qe, *toks)
        if self.dma_cnt[ch] > 0:
            self._emit_wait(qe, ch, self.dma_cnt[ch])
        self.dma_cnt[ch] += 16
        v = self.dma_cnt[ch]
        sem = self.sem[ch]
        if accum:
            self.prog[qe].append(
                lambda eng, o=out_ap, i=in_ap, sem=sem:
                    eng.dma_start(o, i, accum_op=ALU.add).then_inc(sem, 16))
        else:
            self.prog[qe].append(
                lambda eng, o=out_ap, i=in_ap, sem=sem:
                    eng.dma_start(o, i).then_inc(sem, 16))
        tok = (ch, v)
        self._note_read(reads, tok)
        self._note_write(writes, tok)
        return tok

    def mm(self, mms, reads=(), writes=(), inc=True):
        toks = self._read_tokens(reads) + self._write_tokens(writes)
        self.wait("T", *toks)
        tok = None
        for i, m in enumerate(mms):
            out_ap, lhsT, rhs, start, stop = m[:5]
            kw = m[5] if len(m) > 5 else {}
            last = inc and (i == len(mms) - 1)
            t2 = self._raw_op(
                "T",
                lambda eng, o=out_ap, l=lhsT, r=rhs, s0=start, s1=stop, kw=kw:
                    eng.matmul(o, l, r, start=s0, stop=s1,
                               skip_group_check=True, **kw),
                last)
            if last:
                tok = t2
        self._note_read(reads, tok)
        self._note_write(writes, tok)
        return tok

    def retag(self, name):
        toks = self._write_tokens([name])
        for r in self._rel(name):
            del self.regions[r]
        self.regions[name] = (None, [t for t in toks if t is not None])

    def token(self, e):
        return (e, self.cnt[e])

    def build(self):
        nc = self.nc
        with nc.Block() as block:
            for e, starter in (("Q", block.sync), ("G", block.gpsimd),
                               ("T", block.tensor), ("V", block.vector),
                               ("S", block.scalar)):
                fns = self.prog[e]
                if not fns:
                    continue

                def body(eng, fns=fns):
                    for f in fns:
                        f(eng)
                starter(body)


def _fm(w):
    d, f = w.shape
    c = (d + 127) // 128
    if d % 128:
        wp = np.zeros((c * 128, f), w.dtype)
        wp[:d] = w
        w = wp
    return np.ascontiguousarray(w.reshape(c, 128, f).transpose(1, 0, 2))


def _pp(v):
    d = v.shape[-1]
    c = (d + 127) // 128
    if d % 128:
        vp = np.zeros(v.shape[:-1] + (c * 128,), v.dtype)
        vp[..., :d] = v
        v = vp
    return np.ascontiguousarray(
        np.moveaxis(v.reshape(v.shape[:-1] + (c, 128)), -1, 0))


def _swap_cols_per_head(w, dh=64):
    f = w.shape[-1]
    idx = np.arange(f)
    h = idx // dh
    r = idx % dh
    perm = h * dh + (r + dh // 2) % dh
    return w[..., perm]


def _interleave128(a, b):
    d, n = a.shape
    k = n // 128
    out = np.empty((d, 2 * n), a.dtype)
    for i in range(k):
        out[:, 256 * i:256 * i + 128] = a[:, 128 * i:128 * i + 128]
        out[:, 256 * i + 128:256 * i + 256] = b[:, 128 * i:128 * i + 128]
    return out


def _rope_maps_l0():
    dh, window = 64, 512
    half_idx = np.arange(0, dh, 2, dtype=np.float64)
    inv_freq = 1.0 / 10000.0 ** (half_idx / dh)
    t = np.arange(T, dtype=np.float64)
    freqs = t[:, None] * inv_freq[None, :]
    freqs = np.concatenate([freqs, freqs], -1)
    base = (half_idx + 0.4 * dh) / (1.4 * dh)
    power = (t - T // 2) / float(window // 2)
    scale = base[None, :] ** power[:, None]
    scale = np.concatenate([scale, scale], -1)
    cos = np.cos(freqs)
    sin = np.sin(freqs)
    sgn = np.where(np.arange(dh) < dh // 2, -1.0, 1.0)
    aq = (cos * scale * 8.0).T
    bq = (sgn[None, :] * sin * scale * 8.0).T
    ak = (cos / scale).T
    bk = (sgn[None, :] * sin / scale).T
    return [np.ascontiguousarray(np.tile(m, (2, 1)).astype(np.float32))
            for m in (aq, bq, ak, bk)]


def _rope_maps_l1f():
    half = 32
    inv = 1.0 / 10000.0 ** (np.arange(half, dtype=np.float64) / half)
    ang = np.arange(T, dtype=np.float64)[:, None] * inv[None, :]
    cos = np.cos(ang)
    sin = np.sin(ang)
    a = np.concatenate([cos, cos], -1)
    b = np.concatenate([-sin, sin], -1)
    return [np.ascontiguousarray(np.tile(m, (2, 1)).astype(np.float32))
            for m in ((a.T * 0.125), (b.T * 0.125), a.T, b.T)]


def _gumbel_noise():
    import jax
    cpu = jax.devices("cpu")[0]
    with jax.default_device(cpu):
        import jax.numpy as jnp
        n = jax.random.gumbel(jax.random.key(42), (8, T, 8), jnp.float32) * 0.05
        return np.asarray(n)


def _stage(inputs):
    f32 = np.float32
    sh = {}
    x = np.asarray(inputs["L0_attn_x"], f32)
    percore = []
    noise = _gumbel_noise()
    for b in range(NCORES):
        xc = np.ascontiguousarray(x[b].T)
        percore.append({
            "x_st": _fm(xc),
            "noise_st": np.ascontiguousarray(
                noise[b].reshape(NTT, 128, 8).transpose(1, 0, 2).astype(f32)),
        })

    sh["c_ones"] = np.ones((128, 128), f32)
    o64 = np.zeros((128, 128), f32)
    o64[:64, :64] = 1.0
    o64[64:, 64:] = 1.0
    sh["c_o64"] = o64
    sh["c_id"] = np.eye(128, dtype=f32)
    md = np.where(np.arange(128)[None, :] <= np.arange(128)[:, None], 0.0, NEG)
    sh["c_maskd"] = md.astype(f32)
    j = np.arange(640)[None, :]
    p = np.arange(128)[:, None]
    ml = np.where((j >= p) & (j <= p + 512), 0.0, NEG)
    sh["c_maskl"] = ml.astype(f32)
    sel = np.zeros((8, 8 * 128), f32)
    for e in range(8):
        sel[e, e * 128:(e + 1) * 128] = 1.0
    sh["c_sel"] = sel
    sh["c_id16"] = (np.eye(128) / 16.0).astype(f32)
    r0 = _rope_maps_l0()
    sh["r0aq"], sh["r0bq"], sh["r0ak"], sh["r0bk"] = r0
    r1 = _rope_maps_l1f()
    sh["r1aq"], sh["r1bq"], sh["r1ak"], sh["r1bk"] = r1
    pos = np.arange(T, dtype=f32)
    rel = np.exp(-np.abs(pos[:, None] - pos[None, :]) / 16.0).astype(f32)
    sh["rel_st"] = np.ascontiguousarray(rel.reshape(NTT, 128, T))

    wqkv = np.asarray(inputs["L0_ffn_wqkv"], f32)
    parts = []
    for l in range(2):
        q = wqkv[l][:, 0:512]
        k = wqkv[l][:, 512:1024]
        v = wqkv[l][:, 1024:1536]
        qx = _interleave128(q, _swap_cols_per_head(q))
        kx = _interleave128(k, _swap_cols_per_head(k))
        parts.append(_fm(np.concatenate([qx, kx, v], -1)))
    sh["w_l0_qkv"] = np.stack(parts)
    sh["w_l0_wo"] = np.stack([_fm(np.asarray(inputs["L0_ffn_wo"], f32)[l])
                              for l in range(2)])
    w1 = np.asarray(inputs["L0_ffn_w1"], f32)
    sh["w_l0_w1a"] = np.stack([_fm(w1[l][:, :2730]) for l in range(2)])
    sh["w_l0_w1g"] = np.stack([_fm(w1[l][:, 2730:]) for l in range(2)])
    w2 = np.asarray(inputs["L0_ffn_w2"], f32)
    sh["w_l0_w2"] = np.stack([_fm(w2[l]) for l in range(2)])
    qs = np.asarray(inputs["L0_ffn_q_scale"], f32)
    ks = np.asarray(inputs["L0_ffn_k_scale"], f32)
    qst = np.tile(qs, (1, 2)).T
    kst = np.tile(ks, (1, 2)).T
    swi = np.concatenate([np.arange(32, 64), np.arange(0, 32),
                          np.arange(96, 128), np.arange(64, 96)])
    sh["p_l0_qs"] = np.stack([qst, qst[swi]], -1).astype(f32)
    sh["p_l0_ks"] = np.stack([kst, kst[swi]], -1).astype(f32)
    sh["p_l0_lng"] = np.stack([_pp(np.asarray(inputs["L0_ffn_ln_g"], f32)[l])
                               for l in range(2)], 1)
    sh["p_l0_lnb"] = np.stack([_pp(np.asarray(inputs["L0_ffn_ln_b"], f32)[l])
                               for l in range(2)], 1)
    b1 = np.asarray(inputs["L0_ffn_b1"], f32)
    sh["p_l0_b1a"] = np.stack([_pp(b1[l][:2730]) for l in range(2)], 1)
    sh["p_l0_b1g"] = np.stack([_pp(b1[l][2730:]) for l in range(2)], 1)
    sh["p_l0_b2"] = np.stack([_pp(np.asarray(inputs["L0_ffn_b2"], f32)[l])
                              for l in range(2)], 1)

    sh["w_l0a_q"] = _fm(np.ascontiguousarray(np.asarray(inputs["L0_attn_Wq"], f32).T))
    sh["w_l0a_o"] = _fm(np.ascontiguousarray(np.asarray(inputs["L0_attn_Wo"], f32).T))
    sh["p_l0a_bo"] = _pp(np.asarray(inputs["L0_attn_bo"], f32))

    wq = np.asarray(inputs["L1_ffn_wq"], f32)
    wk = np.asarray(inputs["L1_ffn_wk"], f32)
    sh["w_l1f_q"] = _fm(_interleave128(wq, _swap_cols_per_head(wq)))
    sh["w_l1f_k"] = _fm(_interleave128(wk, _swap_cols_per_head(wk)))
    sh["w_l1f_v"] = _fm(np.asarray(inputs["L1_ffn_wv"], f32))
    sh["w_l1f_o"] = _fm(np.asarray(inputs["L1_ffn_wo"], f32))
    sh["w_l1f_r"] = _fm(np.asarray(inputs["L1_ffn_router_w"], f32))
    for nm, key in (("e1", "L1_ffn_w1"), ("e2", "L1_ffn_w2"), ("eo", "L1_ffn_wout")):
        w = np.asarray(inputs[key], f32)
        sh["w_l1f_" + nm] = np.stack([_fm(w[e]) for e in range(8)])
    sh["p_l1f_s1"] = _pp(np.asarray(inputs["L1_ffn_scale1"], f32))
    sh["p_l1f_s2"] = _pp(np.asarray(inputs["L1_ffn_scale2"], f32))
    sh["p_l1f_b1"] = np.stack([_pp(np.asarray(inputs["L1_ffn_b1"], f32)[e])
                               for e in range(8)], 1)
    sh["p_l1f_b2"] = np.stack([_pp(np.asarray(inputs["L1_ffn_b2"], f32)[e])
                               for e in range(8)], 1)
    sh["w_l1f_bo"] = np.asarray(inputs["L1_ffn_bout"], f32)

    for nm, key in (("q", "L1_attn_Wq"), ("k", "L1_attn_Wk"),
                    ("1", "L1_attn_W1"), ("2", "L1_attn_W2")):
        w = np.asarray(inputs[key], f32)
        sh["w_l1a_" + nm] = np.stack(
            [_fm(np.ascontiguousarray(w[l].T)) for l in range(6)])
    for nm, key in (("bq", "L1_attn_bq"), ("bk", "L1_attn_bk"),
                    ("l1w", "L1_attn_ln1_w"), ("l1b", "L1_attn_ln1_b"),
                    ("l2w", "L1_attn_ln2_w"), ("l2b", "L1_attn_ln2_b"),
                    ("l3w", "L1_attn_ln3_w"), ("l3b", "L1_attn_ln3_b"),
                    ("b2", "L1_attn_b2")):
        v = np.asarray(inputs[key], f32)
        sh["p_l1a_" + nm] = np.stack([_pp(v[l]) for l in range(6)], 1)
    b1 = np.asarray(inputs["L1_attn_b1"], f32)
    sh["p_l1a_b1"] = np.stack([_pp(b1[l]) for l in range(6)], 1)
    return sh, percore


class Prog:
    def __init__(self, nc, nblocks):
        self.nc = nc
        self.kb = KB(nc)
        self.nblocks = nblocks
        self.ios = {}
        self._trctr = 0

    def build(self, shapes):
        nc = self.nc
        kb = self.kb
        st = kb.stack

        for name, arr in shapes.items():
            self.ios[name] = nc.dram_tensor(name, list(arr.shape), F32,
                                            kind="ExternalInput")
        x_io = nc.dram_tensor("x_out", [128, C, T], F32, kind="ExternalOutput")
        self.x_io = x_io

        def sb(name, shape, dt=F32):
            return st.enter_context(nc.sbuf_tensor(name, shape, dt))

        self.ST = sb("ST", [128, C, T], F32)
        self.XB = sb("XB", [128, 16384], BF16)
        self.YB = sb("YB", [128, 16384], BF16)
        self.ZB = sb("ZB", [128, 16384], BF16)
        self.WAR = sb("WAR", [128, 8192], BF16)
        self.PL = sb("PL", [128, 2048], F32)
        self.ATR = sb("ATR", [128, 2048], BF16)
        self.SML = sb("SML", [128, 128], F32)
        self.SQB = sb("SQB", [128, 2, 512], BF16)
        self.SPR = sb("SPR", [128, 256], F32)
        self.MAPS = sb("MAPS", [128, 2, T], BF16)
        self.GRW = self.MAPS[0:8, 0, :]
        self.BOUT = self.MAPS[0:8, 1, 0:1024]
        self.CONST = sb("CONST", [128, 2176], BF16)
        self.CID = sb("CID", [128, 128], F32)
        self.CF = sb("CF", [128, 4], F32)
        self.PSF = st.enter_context(nc.psum_tensor("PSF", [128, 8, 512], F32))

        self.SC0 = self.PL[:, 0:512]
        self.SC1 = self.PL[:, 512:1024]
        self.STAm = self.PL[:, 1024:1536]
        self.STAr = self.PL[:, 1536:2048]
        self.AUS = self.PL[:, 0:2048]

        self._wchs = [kb.channel() for _ in range(4)]
        self._wi = 0
        self.cX = kb.channel()
        self.cR = kb.channel()
        self.cA = kb.channel()

        self._load_consts()
        kb.do("V", lambda v: v.memset(self.CF[:, 0:1], 1e-5), writes=("CF",))
        kb.do("V", lambda v: v.memset(self.CF[:, 1:2], 1e-6), writes=("CF",))
        kb.do("V", lambda v: v.memset(self.CF[:, 2:3], 1e-24), writes=("CF",))
        kb.dma(self.cX, "Q", x_io.ap(), self.ios["x_st"].ap(), writes=("XIO",))

        blocks = [
            (self.b_l0ffn, 0.5), (self.b_l0attn, 1.0), (self.b_l0ffn, 0.5),
            (self.b_l1ffn, 0.5), (self.b_l1attn, 1.0), (self.b_l1ffn, 0.5),
        ]
        for bi in range(self.nblocks):
            fn, sc = blocks[bi]
            self.tln(bi)
            fn(bi)
            self.residual_out(sc)

        tok = kb._write_tokens(["XIO"]) + kb._read_tokens(["XIO"])
        kb.wait("Q", *tok)
        kb.build()

    @property
    def cW(self):
        ch = self._wchs[self._wi % 4]
        self._wi += 1
        return ch

    def _load_consts(self):
        kb = self.kb
        io = self.ios
        CB = self.CONST
        kb.dma(self.cR, "Q", self.CID[:], io["c_id"].ap(), writes=("CID",))
        kb.dma(self.cW, "G", CB[:, 0:128], io["c_ones"].ap(), writes=("CST/ones",))
        kb.dma(self.cW, "G", CB[:, 128:256], io["c_o64"].ap(), writes=("CST/o64",))
        kb.dma(self.cW, "G", CB[:, 256:384], io["c_maskd"].ap(), writes=("CST/maskd",))
        kb.dma(self.cW, "G", CB[:, 384:1024], io["c_maskl"].ap(), writes=("CST/maskl",))
        kb.dma(self.cW, "G", CB[0:8, 1024:2048], io["c_sel"].ap(), writes=("CST/sel",))
        kb.dma(self.cW, "G", CB[:, 2048:2176], io["c_id16"].ap(), writes=("CST/id16",))

    def c_ones(self):
        return self.CONST[:, 0:128]

    def c_o64(self):
        return self.CONST[:, 128:256]

    def c_maskd(self):
        return self.CONST[:, 256:384]

    def c_maskl(self):
        return self.CONST[:, 384:1024]

    def c_sel(self, e):
        return self.CONST[0:8, 1024 + e * 128:1024 + (e + 1) * 128]

    def c_id16(self):
        return self.CONST[:, 2048:2176]

    def _stats(self, tg, src, eps, with_mean, nfeat=1024.0):
        kb = self.kb
        PSF = self.PSF
        sl = slice(tg * TGS, (tg + 1) * TGS)
        srd = ("%s/%d" % (self._srcname, tg),)
        if with_mean:
            kb.mm([(PSF[:, 0, :], self.c_ones(), src[:, c, sl], c == 0, c == C - 1)
                   for c in range(C)], reads=srd + ("CST/ones",), writes=("PB0",))
        for c in range(C):
            kb.do("S", lambda s, c=c: s.activation(
                self.SQB[:, c % 2, :], src[:, c, sl], AF.Square),
                  reads=srd, writes=("SQB/%d" % (c % 2),))
            kb.mm([(PSF[:, 1, :], self.c_ones(), self.SQB[:, c % 2, :],
                    c == 0, c == C - 1)],
                  reads=("SQB/%d" % (c % 2), "CST/ones"), writes=("PB1",))
        inv = 1.0 / nfeat
        if with_mean:
            kb.do("V", lambda v: v.tensor_scalar_mul(self.STAm, PSF[:, 0, :], inv),
                  reads=("PB0",), writes=("PL/m",))
            kb.do("S", lambda s: s.activation(self.SC0, self.STAm, AF.Square),
                  reads=("PL/m",), writes=("PL/s0",))
            kb.do("V", lambda v: v.scalar_tensor_tensor(
                self.SC1, PSF[:, 1, :], inv, self.SC0, ALU.mult, ALU.subtract),
                  reads=("PB1", "PL/s0"), writes=("PL/s1",))
        else:
            kb.do("V", lambda v: v.tensor_scalar_mul(self.SC1, PSF[:, 1, :], inv),
                  reads=("PB1",), writes=("PL/s1",))
        kb.do("S", lambda s: s.activation(self.SC0, self.SC1, AF.Ln, bias=eps),
              reads=("PL/s1", "CF"), writes=("PL/s0",))
        kb.do("S", lambda s: s.activation(self.STAr, self.SC0, AF.Exp, scale=-0.5),
              reads=("PL/s0",), writes=("PL/r",))

    def tln(self, bi):
        kb = self.kb
        for b in ("SML", "PL", "ATR", "MAPS"):
            kb.retag(b)
        ST = self.ST
        XB = self.XB.rearrange("p (c t) -> p c t", t=T)
        x_ap = self.x_io.ap()
        self._srcname = "XB"
        for tg in range(NTG):
            sl = slice(tg * TGS, (tg + 1) * TGS)
            kb.dma(self.cX, "Q", ST[:, :, sl], x_ap[:, :, sl],
                   reads=("XIO",), writes=("ST/%d" % tg,))
            kb.do("V", lambda v, sl=sl: v.tensor_copy(XB[:, :, sl], ST[:, :, sl]),
                  reads=("ST/%d" % tg,), writes=("XB/%d" % tg,))
            self._stats(tg, XB, self.CF[:, 0:1], True)
            for c in range(C):
                s0 = "PL/s%d" % (c % 2)
                scv = self.SC0 if c % 2 == 0 else self.SC1
                kb.do("V", lambda v, c=c, scv=scv, sl=sl: v.tensor_sub(
                    scv, ST[:, c, sl], self.STAm),
                      reads=("ST/%d" % tg, "PL/m"), writes=(s0,))
                kb.do("V", lambda v, c=c, scv=scv, sl=sl: v.tensor_mul(
                    ST[:, c, sl], scv, self.STAr),
                      reads=(s0, "PL/r"), writes=("ST/%d" % tg,))
                kb.do("V", lambda v, c=c, sl=sl: v.tensor_copy(
                    XB[:, c, sl], ST[:, c, sl]),
                      reads=("ST/%d" % tg,), writes=("XB/%d" % tg,))

    def residual_out(self, scale):
        kb = self.kb
        ST = self.ST
        if scale != 1.0:
            for tg in range(NTG):
                sl = slice(tg * TGS, (tg + 1) * TGS)
                kb.do("V", lambda v, sl=sl: v.tensor_scalar_mul(
                    ST[:, :, sl], ST[:, :, sl], scale),
                      reads=(), writes=("ST/%d" % tg,))
        for tg in range(NTG):
            sl = slice(tg * TGS, (tg + 1) * TGS)
            kb.dma(self.cA, "G", self.x_io.ap()[:, :, sl], self.ST[:, :, sl],
                   reads=("ST/%d" % tg,), writes=("XIO",), accum=True)

    def b_l0ffn(self, bi):
        for l in range(2):
            self._l0ffn_layer(l)

    def _shift_cast(self):
        kb = self.kb
        ST = self.ST
        XB = self.XB.rearrange("p (c t) -> p c t", t=T)
        stwr = tuple("ST/%d" % t for t in range(NTG))
        xbwr = tuple("XB/%d" % t for t in range(NTG))
        kb.do("V", lambda v: v.tensor_copy(XB[:, 0:4, :], ST[:, 0:4, :]),
              reads=stwr, writes=xbwr)
        kb.do("V", lambda v: v.tensor_copy(XB[:, 4:8, 1:T], ST[:, 4:8, 0:T - 1]),
              reads=stwr, writes=xbwr)
        kb.do("V", lambda v: v.memset(XB[:, 4:8, 0:1], 0.0),
              reads=(), writes=("XB/0",))

    def _l0ffn_layer(self, l):
        kb = self.kb
        io = self.ios
        for b in ("YB", "ZB", "WAR"):
            kb.retag(b)
        ST = self.ST
        XB = self.XB.rearrange("p (c t) -> p c t", t=T)
        YB = self.YB.rearrange("p (c t) -> p c t", t=T)
        ZBv = self.ZB.rearrange("p (a b) -> p a b", b=512)
        PSF = self.PSF
        WQr = [self.WAR[:, 0:2048].rearrange("p (c f) -> p c f", f=256),
               self.WAR[:, 2048:4096].rearrange("p (c f) -> p c f", f=256)]
        WV = self.WAR[:, 4096:8192].rearrange("p (c f) -> p c f", f=512)
        WOv = self.WAR[:, 4096:8192].rearrange("p (c f) -> p c f", f=1024)

        for nm, dst, src in (
                ("SPR/qs", self.SPR[:, 0:2], io["p_l0_qs"].ap()[:, l, :]),
                ("SPR/ks", self.SPR[:, 2:4], io["p_l0_ks"].ap()[:, l, :]),
                ("SPR/lng", self.SPR[:, 8:16], io["p_l0_lng"].ap()[:, l, :]),
                ("SPR/lnb", self.SPR[:, 16:24], io["p_l0_lnb"].ap()[:, l, :]),
                ("SPR/b1a", self.SPR[:, 24:46], io["p_l0_b1a"].ap()[:, l, :]),
                ("SPR/b1g", self.SPR[:, 46:68], io["p_l0_b1g"].ap()[:, l, :]),
                ("SPR/b2", self.SPR[:, 68:76], io["p_l0_b2"].ap()[:, l, :])):
            kb.dma(self.cR, "Q", dst, src, writes=(nm,))

        self._shift_cast()

        mapq = self.MAPS
        wsrc = io["w_l0_qkv"].ap()
        for qk in range(2):
            kb.dma(self.cW, "G", mapq[:, 0, :],
                   io["r0aq" if qk == 0 else "r0ak"].ap(), writes=("MAPS/0",))
            kb.dma(self.cW, "G", mapq[:, 1, :],
                   io["r0bq" if qk == 0 else "r0bk"].ap(), writes=("MAPS/1",))
            scp = "SPR/qs" if qk == 0 else "SPR/ks"
            scap = self.SPR[:, 0:1] if qk == 0 else self.SPR[:, 2:3]
            scap_s = self.SPR[:, 1:2] if qk == 0 else self.SPR[:, 3:4]
            for oc in range(4):
                ring = oc % 2
                wv = WQr[ring]
                wreg = "WAR/R%d" % ring
                col = qk * 1024 + oc * 256
                kb.dma(self.cW, "G", wv[:], wsrc[l, :, :, col:col + 256],
                       writes=(wreg,))
                for tg in range(NTG):
                    sl = slice(tg * TGS, (tg + 1) * TGS)
                    b0 = (tg % 2)
                    b1 = 2 + (tg % 2)
                    b2 = 4 + (tg % 2)
                    kb.mm([(PSF[:, b0, :], wv[:, ic, 0:128], XB[:, ic, sl],
                            ic == 0, ic == C - 1) for ic in range(C)],
                          reads=(wreg, "XB/%d" % tg), writes=("PB%d" % b0,))
                    kb.mm([(PSF[:, b1, :], wv[:, ic, 128:256], XB[:, ic, sl],
                            ic == 0, ic == C - 1) for ic in range(C)],
                          reads=(wreg, "XB/%d" % tg), writes=("PB%d" % b1,))
                    sq = self.SQB[:, tg % 2, :]
                    sqr = "SQB/%d" % (tg % 2)
                    kb.do("S", lambda s, b0=b0, sq=sq: s.activation(
                        sq, PSF[:, b0, :], AF.Square),
                          reads=("PB%d" % b0,), writes=(sqr,))
                    kb.mm([(PSF[:, b2, :], self.c_o64(), sq, True, True)],
                          reads=(sqr, "CST/o64"), writes=("PB%d" % b2,))
                    kb.do("S", lambda s, b2=b2: s.activation(
                        self.SC0, PSF[:, b2, :], AF.Ln, bias=self.CF[:, 2:3]),
                          reads=("PB%d" % b2, "CF"), writes=("PL/s0",))
                    kb.do("S", lambda s: s.activation(
                        self.STAr, self.SC0, AF.Exp, scale=-0.5),
                          reads=("PL/s0",), writes=("PL/r",))
                    kb.do("V", lambda v, b0=b0, scap=scap: v.scalar_tensor_tensor(
                        self.SC0, PSF[:, b0, :], scap, self.STAr,
                        ALU.mult, ALU.mult),
                          reads=("PB%d" % b0, "PL/r", scp), writes=("PL/s0",))
                    kb.do("V", lambda v, b1=b1, scap_s=scap_s:
                          v.scalar_tensor_tensor(
                              self.SC1, PSF[:, b1, :], scap_s, self.STAr,
                              ALU.mult, ALU.mult),
                          reads=("PB%d" % b1, "PL/r", scp), writes=("PL/s1",))
                    kb.do("V", lambda v, sl=sl: v.tensor_mul(
                        self.SC0, self.SC0, mapq[:, 0, sl]),
                          reads=("PL/s0", "MAPS/0"), writes=("PL/s0",))
                    kb.do("V", lambda v, sl=sl: v.tensor_mul(
                        self.SC1, self.SC1, mapq[:, 1, sl]),
                          reads=("PL/s1", "MAPS/1"), writes=("PL/s1",))
                    yc = qk * 4 + oc
                    kb.do("V", lambda v, yc=yc, sl=sl: v.tensor_add(
                        YB[:, yc, sl], self.SC0, self.SC1),
                          reads=("PL/s0", "PL/s1"), writes=("YB/%d" % yc,))
        kb.dma(self.cW, "G", WV[:], wsrc[l, :, :, 2048:2560], writes=("WAR/C",))
        for tt in range(NTT):
            bk = 6 + tt % 2
            kb.mm([(PSF[:, bk, :], XB[:, ic, tt * 128:(tt + 1) * 128],
                    WV[:, ic, :], ic == 0, ic == C - 1) for ic in range(C)],
                  reads=("WAR/C", "XB/%d" % (tt // 4)), writes=("PB%d" % bk,))
            kb.do("V", lambda v, tt=tt, bk=bk: v.tensor_copy(
                ZBv[:, tt, :], PSF[:, bk, :]),
                  reads=("PB%d" % bk,), writes=("ZB/%d" % tt,))

        kb.dma(self.cW, "G", WOv[:, 0:4, :], io["w_l0_wo"].ap()[l],
               writes=("WAR/C",))

        ctx = XB
        heads = []
        for h in range(8):
            qc = h // 2
            qpart = slice((h % 2) * 64, (h % 2) * 64 + 64)
            heads.append(dict(
                qsrc=lambda qt, qpart=qpart, qc=qc:
                    YB[qpart, qc, qt * 128:(qt + 1) * 128],
                qreads=lambda qt, qc=qc: ("YB/%d" % qc,),
                ksrc=lambda c0, n, qpart=qpart, qc=qc:
                    YB[qpart, 4 + qc, c0:c0 + n],
                kreads=("YB/%d" % (4 + qc),),
                vsrc=lambda kt, h=h: ZBv[:, kt, h * 64:(h + 1) * 64],
                vreads=lambda kt: ("ZB/%d" % kt,),
                ctx=lambda qt, h=h: ctx[(h % 2) * 64:(h % 2) * 64 + 64,
                                        h // 2, qt * 128:(qt + 1) * 128],
                ctx_wr=lambda qt: ("XB/%d" % (qt // 4),),
                local=True, scale=1.0, nqt=NTT))
        self._attn_run(heads)

        for oc in range(C):
            for tg in range(NTG):
                sl = slice(tg * TGS, (tg + 1) * TGS)
                bk = (oc * NTG + tg) % 4
                kb.mm([(PSF[:, bk, :], WOv[:, ic, oc * 128:(oc + 1) * 128],
                        ctx[:, ic, sl], ic == 0, ic == 3) for ic in range(4)],
                      reads=("WAR/C", "XB/%d" % tg), writes=("PB%d" % bk,))
                kb.do("V", lambda v, oc=oc, sl=sl, bk=bk: v.tensor_add(
                    ST[:, oc, sl], ST[:, oc, sl], PSF[:, bk, :]),
                      reads=("PB%d" % bk,), writes=("ST/%d" % tg,))

        self._shift_cast()
        self._srcname = "XB"
        for tg in range(NTG):
            sl = slice(tg * TGS, (tg + 1) * TGS)
            self._stats(tg, XB, self.CF[:, 0:1], True)
            for c in range(C):
                s0 = "PL/s%d" % (c % 2)
                scv = self.SC0 if c % 2 == 0 else self.SC1
                kb.do("V", lambda v, c=c, scv=scv, sl=sl: v.tensor_sub(
                    scv, XB[:, c, sl], self.STAm),
                      reads=("XB/%d" % tg, "PL/m"), writes=(s0,))
                kb.do("V", lambda v, c=c, scv=scv: v.scalar_tensor_tensor(
                    scv, scv, self.SPR[:, 8 + c:9 + c], self.STAr,
                    ALU.mult, ALU.mult),
                      reads=(s0, "SPR/lng", "PL/r"), writes=(s0,))
                kb.do("V", lambda v, c=c, scv=scv, sl=sl: v.tensor_scalar_add(
                    XB[:, c, sl], scv, self.SPR[:, 16 + c:17 + c]),
                      reads=(s0, "SPR/lnb"), writes=("XB/%d" % tg,))

        kb.retag("YB")
        w1a_sl = [self.WAR[:, 0:1024].rearrange("p (c f) -> p c f", f=128),
                  self.WAR[:, 1024:2048].rearrange("p (c f) -> p c f", f=128)]
        w1g_sl = [self.WAR[:, 2048:3072].rearrange("p (c f) -> p c f", f=128),
                  self.WAR[:, 3072:4096].rearrange("p (c f) -> p c f", f=128)]
        w2e = self.WAR[:, 4096:6912].rearrange("p (c f) -> p c f", f=128)
        YBg = self.YB.rearrange("p (a b) -> p a b", b=512)
        for tg in range(NTG):
            sl = slice(tg * TGS, (tg + 1) * TGS)
            for j in range(22):
                m = 128 if j < 21 else 42
                ring = j % 2
                wa = w1a_sl[ring]
                wg = w1g_sl[ring]
                kb.dma(self.cW, "G", wa[:, :, 0:m],
                       io["w_l0_w1a"].ap()[l, :, :, j * 128:j * 128 + m],
                       writes=("WAR/R0/%d" % ring,))
                kb.dma(self.cW, "G", wg[:, :, 0:m],
                       io["w_l0_w1g"].ap()[l, :, :, j * 128:j * 128 + m],
                       writes=("WAR/R1/%d" % ring,))
                ba = (j % 2) * 2
                bg = ba + 1
                kb.mm([(PSF[0:m, ba, :], wa[:, ic, 0:m], XB[:, ic, sl],
                        ic == 0, ic == C - 1) for ic in range(C)],
                      reads=("WAR/R0/%d" % ring, "XB/%d" % tg),
                      writes=("PB%d" % ba,))
                kb.mm([(PSF[0:m, bg, :], wg[:, ic, 0:m], XB[:, ic, sl],
                        ic == 0, ic == C - 1) for ic in range(C)],
                      reads=("WAR/R1/%d" % ring, "XB/%d" % tg),
                      writes=("PB%d" % bg,))
                sq = self.SQB[:, j % 2, :]
                sqr = "SQB/%d" % (j % 2)
                kb.do("S", lambda s, m=m, bg=bg, j=j, sq=sq: s.activation(
                    sq[0:m, :], PSF[0:m, bg, :], AF.Gelu,
                    bias=self.SPR[0:m, 46 + j:47 + j]),
                      reads=("PB%d" % bg, "SPR/b1g"), writes=(sqr,))
                kb.do("V", lambda v, m=m, ba=ba, j=j, sq=sq:
                      v.scalar_tensor_tensor(
                          YBg[0:m, j, :], PSF[0:m, ba, :],
                          self.SPR[0:m, 24 + j:25 + j], sq[0:m, :],
                          ALU.add, ALU.mult),
                      reads=("PB%d" % ba, "SPR/b1a", sqr),
                      writes=("YB/g%d" % j,))
            for oco in range(8):
                kb.dma(self.cW, "G", w2e[:],
                       io["w_l0_w2"].ap()[l, :, :, oco * 128:(oco + 1) * 128],
                       writes=("WAR/C",))
                bk = 4 + (oco % 4)
                mms = []
                for j in range(22):
                    m = 128 if j < 21 else 42
                    mms.append((PSF[:, bk, :], w2e[0:m, j, :],
                                YBg[0:m, j, :], j == 0, j == 21))
                kb.mm(mms, reads=("WAR/C",) + tuple("YB/g%d" % j
                                                    for j in range(22)),
                      writes=("PB%d" % bk,))
                kb.do("V", lambda v, oco=oco, sl=sl, bk=bk:
                      v.scalar_tensor_tensor(
                          ST[:, oco, sl], PSF[:, bk, :],
                          self.SPR[:, 68 + oco:69 + oco], ST[:, oco, sl],
                          ALU.add, ALU.add),
                      reads=("PB%d" % bk, "SPR/b2"), writes=("ST/%d" % tg,))

    def _attn_run(self, heads):
        jobs = []
        ti = 0
        for hd in heads:
            for qt in range(hd["nqt"]):
                k0 = max(0, qt * 128 - 512) if hd["local"] else 0
                k1 = qt * 128 + 128
                cols = k1 - k0
                nh = 1 if cols <= 1024 else 2
                for kh in range(nh):
                    jobs.append(dict(hd=hd, qt=qt, kh=kh, nh=nh, k0=k0, k1=k1,
                                     cols=cols, tpar=ti % 2))
                ti += 1
        prev = None
        for idx, jb in enumerate(jobs):
            self._attn_pA(jb, idx)
            if prev is not None:
                self._attn_pB(prev)
            prev = jb
        self._attn_pB(prev)

    def _attn_pA(self, jb, idx):
        kb = self.kb
        PSF = self.PSF
        hd = jb["hd"]
        qt = jb["qt"]
        h0 = jb["k0"] + jb["kh"] * 1024
        hn = min(1024, jb["k1"] - h0)
        bs = 2 * (idx % 2)
        ng = (hn + 511) // 512
        qs = hd["qsrc"](qt)
        qrd = hd["qreads"](qt)
        for g in range(ng):
            c0 = h0 + g * 512
            n = min(512, jb["k1"] - c0)
            kb.mm([(PSF[:, bs + g, 0:n], qs, hd["ksrc"](c0, n), True, True)],
                  reads=qrd + hd["kreads"], writes=("PB%d" % (bs + g),))
        jb["bs"] = bs
        jb["ng"] = ng
        jb["h0"] = h0
        jb["hn"] = hn

    def _attn_pB(self, jb):
        kb = self.kb
        PSF = self.PSF
        hd = jb["hd"]
        qt = jb["qt"]
        kh = jb["kh"]
        bs, ng, hn = jb["bs"], jb["ng"], jb["hn"]
        scale = hd["scale"]
        o = 64 + jb["tpar"] * 16
        hp2 = "%d" % jb["tpar"]
        full_mask = hd["local"] and qt >= 4
        pbs = tuple("PB%d" % (bs + g) for g in range(ng))
        psh = PSF[:, bs:bs + ng, :].rearrange("p a b -> p (a b)")[:, 0:hn]
        ar = self.AUS[:, kh * 1024:kh * 1024 + hn]
        nmh = self.SML[:, o + kh:o + kh + 1]
        zh = self.SML[:, o + 4 + kh:o + 5 + kh]
        has_diag = (kh == jb["nh"] - 1) and not full_mask
        if full_mask:
            kb.do("V", lambda v, ar=ar, psh=psh, hn=hn: v.tensor_add(
                ar, psh, self.c_maskl()[:, 0:hn]),
                  reads=pbs + ("CST/maskl",), writes=("PL/%d" % kh,))
            src, srd = ar, ("PL/%d" % kh,)
        elif has_diag:
            if hn > 128:
                kb.do("V", lambda v, ar=ar, psh=psh, hn=hn: v.tensor_copy(
                    ar[:, 0:hn - 128], psh[:, 0:hn - 128]),
                      reads=pbs, writes=("PL/%d" % kh,))
            kb.do("V", lambda v, ar=ar, psh=psh, hn=hn: v.tensor_add(
                ar[:, hn - 128:hn], psh[:, hn - 128:hn], self.c_maskd()),
                  reads=pbs + ("CST/maskd",), writes=("PL/%d" % kh,))
            src, srd = ar, ("PL/%d" % kh,)
        else:
            src, srd = psh, pbs
        kb.do("V", lambda v, nmh=nmh, src=src: v.reduce_max(
            nmh, src, axis=AX.X, negate=True),
              reads=srd, writes=("SML/nm%s%d" % (hp2, kh),))
        if scale != 1.0:
            nsc = self.SML[:, o + 2 + kh:o + 3 + kh]
            kb.do("V", lambda v, nmh=nmh, nsc=nsc: v.tensor_scalar_mul(
                nsc, nmh, scale),
                  reads=("SML/nm%s%d" % (hp2, kh),),
                  writes=("SML/ns%s%d" % (hp2, kh),))
            bb, br = nsc, "SML/ns%s%d" % (hp2, kh)
        else:
            bb, br = nmh, "SML/nm%s%d" % (hp2, kh)
        kb.do("S", lambda s, ar=ar, src=src, bb=bb, zh=zh, scale=scale:
              s.activation(ar, src, AF.Exp, bias=bb, scale=scale,
                           accum_out=zh),
              reads=srd + (br,), writes=("PL/%d" % kh,
                                         "SML/z%s%d" % (hp2, kh)))
        if kh < jb["nh"] - 1:
            return
        cols = jb["cols"]
        if jb["nh"] == 1:
            zt = self.SML[:, o + 4:o + 5]
            rzt = self.SML[:, o + 15:o + 16]
            kb.do("V", lambda v, zt=zt, rzt=rzt: v.reciprocal(rzt, zt),
                  reads=("SML/z%s0" % hp2,), writes=("SML/r%s" % hp2,))
            kb.do("V", lambda v, rzt=rzt, cols=cols: v.tensor_scalar_mul(
                self.AUS[:, 0:cols], self.AUS[:, 0:cols], rzt),
                  reads=("PL/0", "SML/r%s" % hp2), writes=("PL/0",))
        else:
            nmv = self.SML[:, o:o + 2]
            zv = self.SML[:, o + 4:o + 6]
            nmm = self.SML[:, o + 6:o + 7]
            dv = self.SML[:, o + 8:o + 10]
            gv = self.SML[:, o + 10:o + 12]
            zw = self.SML[:, o + 12:o + 14]
            zt = self.SML[:, o + 14:o + 15]
            rzt = self.SML[:, o + 15:o + 16]
            rs = ("SML/nm%s0" % hp2, "SML/nm%s1" % hp2)
            zs = ("SML/z%s0" % hp2, "SML/z%s1" % hp2)
            mrg = "SML/mg%s" % hp2
            kb.do("V", lambda v, nmv=nmv, nmm=nmm: v.tensor_reduce(
                nmm, nmv, axis=AX.X, op=ALU.min),
                  reads=rs, writes=(mrg,))
            kb.do("V", lambda v, nmv=nmv, nmm=nmm, dv=dv:
                  v.tensor_scalar_sub(dv, nmv, nmm),
                  reads=rs + (mrg,), writes=(mrg + "d",))
            kb.do("S", lambda s, dv=dv, gv=gv, scale=scale: s.activation(
                gv, dv, AF.Exp, scale=-scale),
                  reads=(mrg + "d",), writes=(mrg + "g",))
            kb.do("V", lambda v, zv=zv, gv=gv, zw=zw: v.tensor_mul(zw, zv, gv),
                  reads=zs + (mrg + "g",), writes=(mrg + "w",))
            kb.do("V", lambda v, zw=zw, zt=zt: v.tensor_reduce(
                zt, zw, axis=AX.X, op=ALU.add),
                  reads=(mrg + "w",), writes=(mrg + "t",))
            kb.do("V", lambda v, zt=zt, rzt=rzt: v.reciprocal(rzt, zt),
                  reads=(mrg + "t",), writes=(mrg + "r",))
            kb.do("V", lambda v, gv=gv, rzt=rzt: v.tensor_scalar_mul(
                gv, gv, rzt),
                  reads=(mrg + "g", mrg + "r"), writes=(mrg + "g",))
            for kk in range(2):
                kn = min(1024, cols - kk * 1024)
                kb.do("V", lambda v, kk=kk, kn=kn, gv=gv: v.tensor_scalar_mul(
                    self.AUS[:, kk * 1024:kk * 1024 + kn],
                    self.AUS[:, kk * 1024:kk * 1024 + kn], gv[:, kk:kk + 1]),
                      reads=("PL/%d" % kk, mrg + "g"), writes=("PL/%d" % kk,))
        nkt = cols // 128
        for j0 in range(0, nkt, 4):
            jn = min(4, nkt - j0)
            trb = 6 + self._trctr % 2
            self._trctr += 1
            kb.mm([(PSF[:, trb, jj * 128:(jj + 1) * 128],
                    self.AUS[:, (j0 + jj) * 128:(j0 + jj + 1) * 128],
                    self.CID[:], jj == 0, jj == jn - 1,
                    dict(is_transpose=True)) for jj in range(jn)],
                  reads=("PL", "CID"), writes=("PB%d" % trb,))
            kb.do("V", lambda v, trb=trb, j0=j0, jn=jn: v.tensor_copy(
                self.ATR[:, j0 * 128:(j0 + jn) * 128],
                PSF[:, trb, 0:jn * 128]),
                  reads=("PB%d" % trb,), writes=("ATR",))
        bk = 4 + jb["tpar"]
        k0 = jb["k0"]
        vr = []
        for j in range(nkt):
            vr.extend(hd["vreads"](k0 // 128 + j))
        kb.mm([(PSF[0:64, bk, 0:128], hd["vsrc"](k0 // 128 + j),
                self.ATR[:, j * 128:(j + 1) * 128],
                j == 0, j == nkt - 1) for j in range(nkt)],
              reads=("ATR",) + tuple(vr), writes=("PB%d" % bk,))
        co = hd["ctx"](qt)
        kb.do("V", lambda v, bk=bk, co=co: v.tensor_copy(
            co, PSF[0:64, bk, 0:128]),
              reads=("PB%d" % bk,), writes=hd["ctx_wr"](qt))


    def b_l0attn(self, bi):
        kb = self.kb
        io = self.ios
        for b in ("YB", "ZB", "WAR"):
            kb.retag(b)
        ST = self.ST
        XB = self.XB.rearrange("p (c t) -> p c t", t=T)
        YB = self.YB.rearrange("p (c t) -> p c t", t=T)
        ZBv = self.ZB.rearrange("p (a b) -> p a b", b=1024)
        PSF = self.PSF
        WARv = self.WAR.rearrange("p (c f) -> p c f", f=1024)

        kb.dma(self.cR, "Q", self.SPR[:, 0:8], io["p_l0a_bo"].ap(),
               writes=("SPR/bo",))
        kb.dma(self.cW, "G", WARv[:], io["w_l0a_q"].ap(), writes=("WAR",))

        for oc in range(C):
            for tg in range(NTG):
                sl = slice(tg * TGS, (tg + 1) * TGS)
                bk = (oc * NTG + tg) % 4
                kb.mm([(PSF[:, bk, :], WARv[:, ic, oc * 128:(oc + 1) * 128],
                        XB[:, ic, sl], ic == 0, ic == C - 1) for ic in range(C)],
                      reads=("WAR", "XB/%d" % tg), writes=("PB%d" % bk,))
                kb.do("V", lambda v, oc=oc, sl=sl, bk=bk: v.tensor_copy(
                    YB[:, oc, sl], PSF[:, bk, :]),
                      reads=("PB%d" % bk,), writes=("YB/%d" % oc,))
        for tt in range(NTT):
            for half in range(2):
                bk = 4 + (tt * 2 + half) % 4
                kb.mm([(PSF[:, bk, :], XB[:, ic, tt * 128:(tt + 1) * 128],
                        WARv[:, ic, half * 512:(half + 1) * 512],
                        ic == 0, ic == C - 1) for ic in range(C)],
                      reads=("WAR", "XB/%d" % (tt // 4)), writes=("PB%d" % bk,))
                kb.do("V", lambda v, tt=tt, half=half, bk=bk: v.tensor_copy(
                    ZBv[:, tt, half * 512:(half + 1) * 512], PSF[:, bk, :]),
                      reads=("PB%d" % bk,), writes=("ZB/%d" % tt,))

        kb.dma(self.cW, "G", WARv[:], io["w_l0a_o"].ap(), writes=("WAR",))

        ctx = XB
        heads = []
        for h in range(16):
            qc = h // 2
            qpart = slice((h % 2) * 64, (h % 2) * 64 + 64)
            heads.append(dict(
                qsrc=lambda qt, qpart=qpart, qc=qc:
                    YB[qpart, qc, qt * 128:(qt + 1) * 128],
                qreads=lambda qt, qc=qc: ("YB/%d" % qc,),
                ksrc=lambda c0, n, qpart=qpart, qc=qc:
                    YB[qpart, qc, c0:c0 + n],
                kreads=("YB/%d" % qc,),
                vsrc=lambda kt, h=h: ZBv[:, kt, h * 64:(h + 1) * 64],
                vreads=lambda kt: ("ZB/%d" % kt,),
                ctx=lambda qt, h=h: ctx[(h % 2) * 64:(h % 2) * 64 + 64,
                                        h // 2, qt * 128:(qt + 1) * 128],
                ctx_wr=lambda qt: ("XB/%d" % (qt // 4),),
                local=False, scale=0.125, nqt=NTT))
        self._attn_run(heads)

        for oc in range(C):
            for tg in range(NTG):
                sl = slice(tg * TGS, (tg + 1) * TGS)
                bk = (oc * NTG + tg) % 4
                kb.mm([(PSF[:, bk, :], WARv[:, ic, oc * 128:(oc + 1) * 128],
                        ctx[:, ic, sl], ic == 0, ic == C - 1) for ic in range(C)],
                      reads=("WAR", "XB/%d" % tg), writes=("PB%d" % bk,))
                kb.do("S", lambda s, oc=oc, sl=sl, bk=bk: s.add(
                    ST[:, oc, sl], PSF[:, bk, :], self.SPR[:, oc:oc + 1]),
                      reads=("PB%d" % bk, "SPR/bo"), writes=("ST/%d" % tg,))

    def b_l1ffn(self, bi):
        kb = self.kb
        io = self.ios
        for b in ("YB", "ZB", "WAR"):
            kb.retag(b)
        ST = self.ST
        XB = self.XB.rearrange("p (c t) -> p c t", t=T)
        YB = self.YB.rearrange("p (c t) -> p c t", t=T)
        ZBv = self.ZB.rearrange("p (a b) -> p a b", b=1024)
        PSF = self.PSF
        WARv = self.WAR.rearrange("p (c f) -> p c f", f=1024)
        QKH = self.WAR[:, 4096:8192].rearrange("p (a t) -> p a t", t=T)
        WQr = [self.WAR[:, 0:2048].rearrange("p (c f) -> p c f", f=256),
               self.WAR[:, 2048:4096].rearrange("p (c f) -> p c f", f=256)]

        kb.dma(self.cR, "Q", self.SPR[:, 0:8], io["p_l1f_s1"].ap(),
               writes=("SPR/s1",))
        kb.dma(self.cR, "Q", self.SPR[:, 8:16], io["p_l1f_s2"].ap(),
               writes=("SPR/s2",))

        self._srcname = "XB"
        for tg in range(NTG):
            sl = slice(tg * TGS, (tg + 1) * TGS)
            self._stats(tg, XB, self.CF[:, 1:2], False)
            for c in range(C):
                kb.do("V", lambda v, c=c, sl=sl: v.scalar_tensor_tensor(
                    XB[:, c, sl], ST[:, c, sl], self.SPR[:, c:c + 1],
                    self.STAr, ALU.mult, ALU.mult),
                      reads=("ST/%d" % tg, "SPR/s1", "PL/r"),
                      writes=("XB/%d" % tg,))

        kb.dma(self.cW, "G", WARv[:], io["w_l1f_v"].ap(), writes=("WAR",))
        for tt in range(NTT):
            for half in range(2):
                bk = (tt * 2 + half) % 4
                kb.mm([(PSF[:, bk, :], XB[:, ic, tt * 128:(tt + 1) * 128],
                        WARv[:, ic, half * 512:(half + 1) * 512],
                        ic == 0, ic == C - 1) for ic in range(C)],
                      reads=("WAR", "XB/%d" % (tt // 4)), writes=("PB%d" % bk,))
                kb.do("V", lambda v, tt=tt, half=half, bk=bk: v.tensor_copy(
                    ZBv[:, tt, half * 512:(half + 1) * 512], PSF[:, bk, :]),
                      reads=("PB%d" % bk,), writes=("ZB/%d" % tt,))

        mapq = self.MAPS
        kb.dma(self.cW, "G", mapq[:, 0, :], io["r1ak"].ap(), writes=("MAPS/0",))
        kb.dma(self.cW, "G", mapq[:, 1, :], io["r1bk"].ap(), writes=("MAPS/1",))
        ctx = YB
        for hp in range(8):
            for qk in range(2):
                wsrc = io["w_l1f_q" if qk == 0 else "w_l1f_k"].ap()
                ring = qk
                wv = WQr[ring]
                wreg = "WAR/R%d" % ring
                kb.dma(self.cW, "G", wv[:], wsrc[:, :, hp * 256:hp * 256 + 256],
                       writes=(wreg,))
                for tg in range(NTG):
                    sl = slice(tg * TGS, (tg + 1) * TGS)
                    b0 = tg % 2
                    b1 = 2 + tg % 2
                    kb.mm([(PSF[:, b0, :], wv[:, ic, 0:128], XB[:, ic, sl],
                            ic == 0, ic == C - 1) for ic in range(C)],
                          reads=(wreg, "XB/%d" % tg), writes=("PB%d" % b0,))
                    kb.mm([(PSF[:, b1, :], wv[:, ic, 128:256], XB[:, ic, sl],
                            ic == 0, ic == C - 1) for ic in range(C)],
                          reads=(wreg, "XB/%d" % tg), writes=("PB%d" % b1,))
                    kb.do("V", lambda v, b0=b0, sl=sl: v.tensor_mul(
                        self.SC0, PSF[:, b0, :], mapq[:, 0, sl]),
                          reads=("PB%d" % b0, "MAPS/0"), writes=("PL/s0",))
                    kb.do("V", lambda v, b1=b1, sl=sl: v.tensor_mul(
                        self.SC1, PSF[:, b1, :], mapq[:, 1, sl]),
                          reads=("PB%d" % b1, "MAPS/1"), writes=("PL/s1",))
                    kb.do("V", lambda v, qk=qk, sl=sl: v.tensor_add(
                        QKH[:, qk, sl], self.SC0, self.SC1),
                          reads=("PL/s0", "PL/s1"), writes=("WAR/H%d" % qk,))
            heads = []
            for hh in range(2):
                h = hp * 2 + hh
                qpart = slice(hh * 64, hh * 64 + 64)
                heads.append(dict(
                    qsrc=lambda qt, qpart=qpart:
                        QKH[qpart, 0, qt * 128:(qt + 1) * 128],
                    qreads=lambda qt: ("WAR/H0",),
                    ksrc=lambda c0, n, qpart=qpart:
                        QKH[qpart, 1, c0:c0 + n],
                    kreads=("WAR/H1",),
                    vsrc=lambda kt, h=h: ZBv[:, kt, h * 64:(h + 1) * 64],
                    vreads=lambda kt: ("ZB/%d" % kt,),
                    ctx=lambda qt, hh=hh, hp=hp: ctx[hh * 64:hh * 64 + 64, hp,
                                                     qt * 128:(qt + 1) * 128],
                    ctx_wr=lambda qt, hp=hp: ("YB/%d" % hp,),
                    local=False, scale=0.125, nqt=NTT))
            self._attn_run(heads)

        kb.dma(self.cW, "G", WARv[:], io["w_l1f_o"].ap(), writes=("WAR",))
        for oc in range(C):
            for tg in range(NTG):
                sl = slice(tg * TGS, (tg + 1) * TGS)
                bk = (oc * NTG + tg) % 4
                kb.mm([(PSF[:, bk, :], WARv[:, ic, oc * 128:(oc + 1) * 128],
                        ctx[:, ic, sl], ic == 0, ic == C - 1) for ic in range(C)],
                      reads=("WAR",) + tuple("YB/%d" % i for i in range(8)),
                      writes=("PB%d" % bk,))
                kb.do("V", lambda v, oc=oc, sl=sl, bk=bk: v.tensor_add(
                    ST[:, oc, sl], ST[:, oc, sl], PSF[:, bk, :]),
                      reads=("PB%d" % bk,), writes=("ST/%d" % tg,))

        for tg in range(NTG):
            sl = slice(tg * TGS, (tg + 1) * TGS)
            kb.do("V", lambda v, sl=sl: v.tensor_copy(XB[:, :, sl], ST[:, :, sl]),
                  reads=("ST/%d" % tg,), writes=("XB/%d" % tg,))
            self._stats(tg, XB, self.CF[:, 1:2], False)
            for c in range(C):
                kb.do("V", lambda v, c=c, sl=sl: v.scalar_tensor_tensor(
                    XB[:, c, sl], ST[:, c, sl], self.SPR[:, 8 + c:9 + c],
                    self.STAr, ALU.mult, ALU.mult),
                      reads=("ST/%d" % tg, "SPR/s2", "PL/r"),
                      writes=("XB/%d" % tg,))

        kb.dma(self.cW, "G", self.WAR[:, 0:64].rearrange("p (c f) -> p c f", f=8),
               io["w_l1f_r"].ap(), writes=("WAR/R0",))
        wrt = self.WAR[:, 0:64].rearrange("p (c f) -> p c f", f=8)
        SM = self.SML
        for tt in range(NTT):
            bk = tt % 2
            kb.mm([(PSF[:, bk, 0:8], XB[:, ic, tt * 128:(tt + 1) * 128],
                    wrt[:, ic, :], ic == 0, ic == C - 1) for ic in range(C)],
                  reads=("WAR/R0", "XB/%d" % (tt // 4)), writes=("PB%d" % bk,))
            lg = SM[:, 8:16]
            nr = "SML/t%d" % (tt % 2)
            nsv = self.SPR[:, 144 + (tt % 2) * 8:152 + (tt % 2) * 8]
            kb.dma(self.cR, "Q", nsv, io["noise_st"].ap()[:, tt, :],
                   writes=("SPR/ns%d" % (tt % 2),))
            kb.do("V", lambda v, bk=bk, lg=lg, nsv=nsv: v.tensor_add(
                lg, PSF[:, bk, 0:8], nsv),
                  reads=("PB%d" % bk, "SPR/ns%d" % (tt % 2)), writes=(nr,))
            kb.do("V", lambda v, lg=lg: v.reduce_max(
                SM[:, 16:17], lg, axis=AX.X, negate=True),
                  reads=(nr,), writes=("SML/a",))
            kb.do("S", lambda s, lg=lg: s.activation(
                SM[:, 24:32], lg, AF.Exp, bias=SM[:, 16:17],
                accum_out=SM[:, 17:18]),
                  reads=(nr, "SML/a"), writes=("SML/e8", "SML/z8"))
            kb.do("V", lambda v: v.reciprocal(SM[:, 18:19], SM[:, 17:18]),
                  reads=("SML/z8",), writes=("SML/rz8",))
            kb.do("V", lambda v: v.tensor_scalar_mul(
                SM[:, 8:16], SM[:, 24:32], SM[:, 18:19]),
                  reads=("SML/e8", "SML/rz8"), writes=(nr,))
            kb.do("V", lambda v, lg=lg: v.reduce_max(SM[:, 19:20], lg, axis=AX.X),
                  reads=(nr,), writes=("SML/m1",))
            kb.do("S", lambda s, lg=lg: s.activation(
                SM[:, 32:40], lg, AF.Sign, bias=SM[:, 19:20], scale=-1.0),
                  reads=(nr, "SML/m1"), writes=("SML/u1",))
            kb.do("V", lambda v, lg=lg: v.tensor_mul(
                SM[:, 40:48], lg, SM[:, 32:40]),
                  reads=(nr, "SML/u1"), writes=("SML/p2",))
            kb.do("V", lambda v: v.reduce_max(SM[:, 20:21], SM[:, 40:48],
                                              axis=AX.X),
                  reads=("SML/p2",), writes=("SML/m2",))
            kb.do("S", lambda s, lg=lg: s.activation(
                SM[:, 32:40], lg, AF.Sign, bias=SM[:, 20:21], scale=-1.0),
                  reads=(nr, "SML/m2"), writes=("SML/u1",))
            kb.do("V", lambda v: v.tensor_scalar(
                SM[:, 48:56], SM[:, 32:40], -1.0, 1.0, ALU.mult, ALU.add),
                  reads=("SML/u1",), writes=("SML/f",))
            kb.do("V", lambda v: v.tensor_scalar_min(
                SM[:, 48:56], SM[:, 48:56], 1.0),
                  reads=("SML/f",), writes=("SML/f",))
            kb.do("V", lambda v, lg=lg: v.tensor_mul(
                SM[:, 56:64], lg, SM[:, 48:56]),
                  reads=(nr, "SML/f"), writes=("SML/g",))
            bt = 4 + tt % 2
            kb.mm([(PSF[0:8, bt, 0:128], SM[:, 56:64], self.CID[:],
                    True, True, dict(is_transpose=True))],
                  reads=("SML/g", "CID"), writes=("PB%d" % bt,))
            kb.do("V", lambda v, tt=tt, bt=bt: v.tensor_copy(
                self.GRW[:, tt * 128:(tt + 1) * 128], PSF[0:8, bt, 0:128]),
                  reads=("PB%d" % bt,), writes=("MAPS/g",))
        kb.dma(self.cW, "G", self.BOUT[:], io["w_l1f_bo"].ap(),
               writes=("MAPS/bo",))

        w1sl = [self.WAR[:, 0:1024].rearrange("p (c f) -> p c f", f=128),
                self.WAR[:, 1024:2048].rearrange("p (c f) -> p c f", f=128)]
        w2sl = [self.WAR[:, 2048:3072].rearrange("p (c f) -> p c f", f=128),
                self.WAR[:, 3072:4096].rearrange("p (c f) -> p c f", f=128)]
        wosl = [self.WAR[:, 4096:5120].rearrange("p (c f) -> p c f", f=128),
                self.WAR[:, 5120:6144].rearrange("p (c f) -> p c f", f=128)]
        kb.dma(self.cR, "Q", self.SPR[:, 16:80],
               io["p_l1f_b1"].ap().rearrange("p a b -> p (a b)"),
               writes=("SPR/be1",))
        kb.dma(self.cR, "Q", self.SPR[:, 80:144],
               io["p_l1f_b2"].ap().rearrange("p a b -> p (a b)"),
               writes=("SPR/be2",))
        kb.retag("YB")
        for tg in range(NTG):
            sl = slice(tg * TGS, (tg + 1) * TGS)
            for e in range(8):
                kb.mm([(PSF[:, 6, :], self.c_sel(e), self.GRW[:, sl],
                        True, True)],
                      reads=("CST/sel", "MAPS/g"), writes=("PB6",))
                for oc in range(C):
                    ring = oc % 2
                    b1k = ring
                    b2k = 2 + ring
                    kb.dma(self.cW, "G", w1sl[ring][:],
                           io["w_l1f_e1"].ap()[e, :, :, oc * 128:(oc + 1) * 128],
                           writes=("WAR/R0/%d" % ring,))
                    kb.dma(self.cW, "G", w2sl[ring][:],
                           io["w_l1f_e2"].ap()[e, :, :, oc * 128:(oc + 1) * 128],
                           writes=("WAR/R1/%d" % ring,))
                    kb.mm([(PSF[:, b1k, :], w1sl[ring][:, ic, :], XB[:, ic, sl],
                            ic == 0, ic == C - 1) for ic in range(C)],
                          reads=("WAR/R0/%d" % ring, "XB/%d" % tg),
                          writes=("PB%d" % b1k,))
                    kb.mm([(PSF[:, b2k, :], w2sl[ring][:, ic, :], XB[:, ic, sl],
                            ic == 0, ic == C - 1) for ic in range(C)],
                          reads=("WAR/R1/%d" % ring, "XB/%d" % tg),
                          writes=("PB%d" % b2k,))
                    sq = self.SQB[:, ring, :]
                    sqr = "SQB/%d" % ring
                    kb.do("S", lambda s, b1k=b1k, e=e, oc=oc, sq=sq:
                          s.activation(sq, PSF[:, b1k, :], AF.Silu,
                                       bias=self.SPR[:, 16 + e * 8 + oc:
                                                     17 + e * 8 + oc]),
                          reads=("PB%d" % b1k, "SPR/be1"), writes=(sqr,))
                    scv = self.SC0 if ring == 0 else self.SC1
                    spl = "PL/s%d" % ring
                    kb.do("V", lambda v, b2k=b2k, e=e, oc=oc, sq=sq, scv=scv:
                          v.scalar_tensor_tensor(
                              scv, PSF[:, b2k, :],
                              self.SPR[:, 80 + e * 8 + oc:81 + e * 8 + oc],
                              sq, ALU.add, ALU.mult),
                          reads=("PB%d" % b2k, "SPR/be2", sqr), writes=(spl,))
                    kb.do("V", lambda v, oc=oc, scv=scv: v.tensor_mul(
                        YB[:, oc, 0:TGS], scv, PSF[:, 6, :]),
                          reads=(spl, "PB6"), writes=("YB/%d" % oc,))
                for oc2 in range(C):
                    ring = oc2 % 2
                    bo = 4 + ring
                    kb.dma(self.cW, "G", wosl[ring][:],
                           io["w_l1f_eo"].ap()[e, :, :, oc2 * 128:(oc2 + 1) * 128],
                           writes=("WAR/R2/%d" % ring,))
                    mms = [(PSF[:, bo, :], wosl[ring][:, ic, :], YB[:, ic, 0:TGS],
                            ic == 0, False) for ic in range(C)]
                    if e == 0:
                        mms.append((PSF[:, bo, :],
                                    self.BOUT[:, oc2 * 128:(oc2 + 1) * 128],
                                    self.GRW[:, sl], False, True))
                    else:
                        last = mms.pop()
                        mms.append((last[0], last[1], last[2], last[3], True))
                    rd = ("WAR/R2/%d" % ring,) + tuple("YB/%d" % i
                                                      for i in range(C))
                    if e == 0:
                        rd = rd + ("MAPS/bo", "MAPS/g")
                    kb.mm(mms, reads=rd, writes=("PB%d" % bo,))
                    kb.do("V", lambda v, oc2=oc2, sl=sl, bo=bo: v.tensor_add(
                        ST[:, oc2, sl], ST[:, oc2, sl], PSF[:, bo, :]),
                          reads=("PB%d" % bo,), writes=("ST/%d" % tg,))

    def b_l1attn(self, bi):
        for l in range(6):
            self._l1attn_layer(l)

    def _ln_gb(self, tg, XBv, wof, bof, spw, spb, eps):
        kb = self.kb
        ST = self.ST
        sl = slice(tg * TGS, (tg + 1) * TGS)
        self._srcname = "XB"
        self._stats(tg, XBv, eps, True)
        for c in range(C):
            s0 = "PL/s%d" % (c % 2)
            scv = self.SC0 if c % 2 == 0 else self.SC1
            kb.do("V", lambda v, c=c, scv=scv, sl=sl: v.tensor_sub(
                scv, ST[:, c, sl], self.STAm),
                  reads=("ST/%d" % tg, "PL/m"), writes=(s0,))
            kb.do("V", lambda v, c=c, scv=scv: v.scalar_tensor_tensor(
                scv, scv, self.SPR[:, wof + c:wof + c + 1], self.STAr,
                ALU.mult, ALU.mult),
                  reads=(s0, spw, "PL/r"), writes=(s0,))
            kb.do("V", lambda v, c=c, scv=scv, sl=sl: v.tensor_scalar_add(
                ST[:, c, sl], scv, self.SPR[:, bof + c:bof + c + 1]),
                  reads=(s0, spb), writes=("ST/%d" % tg,))
            kb.do("V", lambda v, c=c, sl=sl: v.tensor_copy(
                XBv[:, c, sl], ST[:, c, sl]),
                  reads=("ST/%d" % tg,), writes=("XB/%d" % tg,))

    def _l1attn_layer(self, l):
        kb = self.kb
        io = self.ios
        for b in ("YB", "ZB", "WAR", "XB"):
            kb.retag(b)
        ST = self.ST
        XB = self.XB.rearrange("p (c t) -> p c t", t=T)
        XBt = self.XB.rearrange("p (a b) -> p a b", b=1024)
        YB = self.YB.rearrange("p (c t) -> p c t", t=T)
        ZB = self.ZB.rearrange("p (c t) -> p c t", t=T)
        PSF = self.PSF
        WQr = [self.WAR[:, 0:1024].rearrange("p (c f) -> p c f", f=128),
               self.WAR[:, 1024:2048].rearrange("p (c f) -> p c f", f=128)]
        WKr = [self.WAR[:, 2048:3072].rearrange("p (c f) -> p c f", f=128),
               self.WAR[:, 3072:4096].rearrange("p (c f) -> p c f", f=128)]

        for nm, cols, key in (("bq", (0, 8), "p_l1a_bq"), ("bk", (8, 16), "p_l1a_bk"),
                              ("l1w", (16, 24), "p_l1a_l1w"),
                              ("l1b", (24, 32), "p_l1a_l1b"),
                              ("l2w", (32, 40), "p_l1a_l2w"),
                              ("l2b", (40, 48), "p_l1a_l2b"),
                              ("l3w", (48, 56), "p_l1a_l3w"),
                              ("l3b", (56, 64), "p_l1a_l3b"),
                              ("b1", (64, 96), "p_l1a_b1"),
                              ("b2", (96, 104), "p_l1a_b2")):
            kb.dma(self.cR, "Q", self.SPR[:, cols[0]:cols[1]],
                   io[key].ap()[:, l, :], writes=("SPR/" + nm,))

        for tg in range(NTG):
            sl = slice(tg * TGS, (tg + 1) * TGS)
            kb.do("V", lambda v, sl=sl: v.tensor_copy(XB[:, :, sl], ST[:, :, sl]),
                  reads=("ST/%d" % tg,), writes=("XB/%d" % tg,))
        for hp in range(8):
            for qk in range(2):
                wr = (WQr if qk == 0 else WKr)[hp % 2]
                wreg = "WAR/%s%d" % ("Q" if qk == 0 else "K", hp % 2)
                kb.dma(self.cW, "G", wr[:],
                       io["w_l1a_q" if qk == 0 else "w_l1a_k"].ap()
                       [l, :, :, hp * 128:(hp + 1) * 128],
                       writes=(wreg,))
                dst = YB if qk == 0 else ZB
                dreg = ("YB/%d" if qk == 0 else "ZB/%d") % hp
                boff = 0 if qk == 0 else 8
                bsp = "SPR/bq" if qk == 0 else "SPR/bk"
                for tg in range(NTG):
                    sl = slice(tg * TGS, (tg + 1) * TGS)
                    bk = (hp * 2 + qk + tg) % 4
                    kb.mm([(PSF[:, bk, :], wr[:, ic, :], XB[:, ic, sl],
                            ic == 0, ic == C - 1) for ic in range(C)],
                          reads=(wreg, "XB/%d" % tg), writes=("PB%d" % bk,))
                    kb.do("S", lambda s, dst=dst, hp=hp, sl=sl, bk=bk, boff=boff:
                          s.add(dst[:, hp, sl], PSF[:, bk, :],
                                self.SPR[:, boff + hp:boff + hp + 1]),
                          reads=("PB%d" % bk, bsp), writes=(dreg,))

        xbwr = tuple("XB/%d" % t for t in range(NTG))
        for kt in range(NTT):
            for ec2 in range(2):
                bk = 4 + ec2
                kb.mm([(PSF[:, bk, jj * 128:(jj + 1) * 128],
                        ST[:, ec2 * 4 + jj, kt * 128:(kt + 1) * 128], self.CID[:],
                        jj == 0, jj == 3, dict(is_transpose=True))
                       for jj in range(4)],
                      reads=tuple("ST/%d" % t for t in range(NTG)) + ("CID",),
                      writes=("PB%d" % bk,))
                kb.do("V", lambda v, kt=kt, ec2=ec2, bk=bk: v.tensor_copy(
                    XBt[:, kt, ec2 * 512:(ec2 + 1) * 512], PSF[:, bk, 0:512]),
                      reads=("PB%d" % bk,), writes=xbwr)

        for qt in range(NTT):
            rmap = self.MAPS[:, qt % 2, :]
            rreg = "MAPS/%d" % (qt % 2)
            kb.dma(self.cW, "G", rmap, io["rel_st"].ap()[qt], writes=(rreg,))
            def emit_qk(h, kh):
                qpart = slice((h % 2) * 64, (h % 2) * 64 + 64)
                qc = h // 2
                for g2 in range(2):
                    kg = kh * 2 + g2
                    kb.mm([(PSF[:, 4 + kg, :],
                            YB[qpart, qc, qt * 128:(qt + 1) * 128],
                            ZB[qpart, qc, kg * 512:(kg + 1) * 512], True, True)],
                          reads=("YB/%d" % qc, "ZB/%d" % qc),
                          writes=("PB%d" % (4 + kg),))

            def emit_half_sm(h, kh):
                o = 4 + (h % 2) * 16
                pbs = ("PB%d" % (4 + kh * 2), "PB%d" % (5 + kh * 2))
                psh = PSF[:, 4 + kh * 2:6 + kh * 2, :].rearrange(
                    "p a b -> p (a b)")
                nmh = self.SML[:, o + kh:o + kh + 1]
                nsc = self.SML[:, o + 2 + kh:o + 3 + kh]
                zh = self.SML[:, o + 4 + kh:o + 5 + kh]
                hp2 = "%d" % (h % 2)
                kb.do("V", lambda v, psh=psh, nmh=nmh: v.reduce_max(
                    nmh, psh, axis=AX.X, negate=True),
                      reads=pbs, writes=("SML/nm%s%d" % (hp2, kh),))
                kb.do("V", lambda v, nmh=nmh, nsc=nsc: v.tensor_scalar_mul(
                    nsc, nmh, 0.125),
                      reads=("SML/nm%s%d" % (hp2, kh),),
                      writes=("SML/ns%s%d" % (hp2, kh),))
                kb.do("S", lambda s, psh=psh, nsc=nsc, zh=zh, kh=kh: s.activation(
                    self.AUS[:, kh * 1024:(kh + 1) * 1024], psh, AF.Exp,
                    bias=nsc, scale=0.125, accum_out=zh),
                      reads=pbs + ("SML/ns%s%d" % (hp2, kh),),
                      writes=("PL/%d" % kh, "SML/z%s%d" % (hp2, kh)))

            def emit_merge(h):
                o = 4 + (h % 2) * 16
                hp2 = "%d" % (h % 2)
                nmv = self.SML[:, o:o + 2]
                zv = self.SML[:, o + 4:o + 6]
                nmm = self.SML[:, o + 6:o + 7]
                dv = self.SML[:, o + 8:o + 10]
                gv = self.SML[:, o + 10:o + 12]
                zw = self.SML[:, o + 12:o + 14]
                zt = self.SML[:, o + 14:o + 15]
                rzt = self.SML[:, o + 15:o + 16]
                rs = ("SML/nm%s0" % hp2, "SML/nm%s1" % hp2)
                zs = ("SML/z%s0" % hp2, "SML/z%s1" % hp2)
                mrg = "SML/mg%s" % hp2
                kb.do("V", lambda v, nmv=nmv, nmm=nmm: v.tensor_reduce(
                    nmm, nmv, axis=AX.X, op=ALU.min),
                      reads=rs, writes=(mrg,))
                kb.do("V", lambda v, nmv=nmv, nmm=nmm, dv=dv:
                      v.tensor_scalar_sub(dv, nmv, nmm),
                      reads=rs + (mrg,), writes=(mrg + "d",))
                kb.do("S", lambda s, dv=dv, gv=gv: s.activation(
                    gv, dv, AF.Exp, scale=-0.125),
                      reads=(mrg + "d",), writes=(mrg + "g",))
                kb.do("V", lambda v, zv=zv, gv=gv, zw=zw: v.tensor_mul(
                    zw, zv, gv),
                      reads=zs + (mrg + "g",), writes=(mrg + "w",))
                kb.do("V", lambda v, zw=zw, zt=zt: v.tensor_reduce(
                    zt, zw, axis=AX.X, op=ALU.add),
                      reads=(mrg + "w",), writes=(mrg + "t",))
                kb.do("V", lambda v, zt=zt, rzt=rzt: v.reciprocal(rzt, zt),
                      reads=(mrg + "t",), writes=(mrg + "r",))
                kb.do("V", lambda v, gv=gv, rzt=rzt: v.tensor_scalar_mul(
                    gv, gv, rzt),
                      reads=(mrg + "g", mrg + "r"), writes=(mrg + "g",))
                for kh in range(2):
                    kb.do("S", lambda s, kh=kh, gv=gv: s.mul(
                        self.ATR[:, kh * 1024:(kh + 1) * 1024],
                        self.AUS[:, kh * 1024:(kh + 1) * 1024],
                        gv[:, kh:kh + 1]),
                          reads=("PL/%d" % kh, mrg + "g"),
                          writes=("ATR/%d" % kh,))

            def emit_ident(h):
                kb.mm([(PSF[:, kg, :], self.c_id16(),
                        self.ATR[:, kg * 512:(kg + 1) * 512],
                        h == 0, h == 15) for kg in range(4)],
                      reads=("ATR", "CST/id16"),
                      writes=("PB0", "PB1", "PB2", "PB3"))

            for h in range(16):
                emit_qk(h, 0)
                emit_half_sm(h, 0)
                emit_qk(h, 1)
                if h > 0:
                    emit_ident(h - 1)
                emit_half_sm(h, 1)
                emit_merge(h)
            emit_ident(15)
            wv = PSF[:, 0:4, :].rearrange("p a b -> p (a b)")
            kb.do("V", lambda v, wv=wv, rmap=rmap: v.tensor_add(
                self.AUS[:], wv, rmap),
                  reads=("PB0", "PB1", "PB2", "PB3", rreg), writes=("PL",))
            nmx = self.SML[:, 0:1]
            zz = self.SML[:, 1:2]
            rz = self.SML[:, 2:3]
            kb.do("V", lambda v: v.reduce_max(nmx, self.AUS[:], axis=AX.X,
                                              negate=True),
                  reads=("PL",), writes=("SML/nm",))
            kb.do("S", lambda s: s.activation(self.AUS[:], self.AUS[:], AF.Exp,
                                              bias=nmx, accum_out=zz),
                  reads=("PL", "SML/nm"), writes=("PL", "SML/z"))
            kb.do("V", lambda v: v.reciprocal(rz, zz),
                  reads=("SML/z",), writes=("SML/rz",))
            kb.do("V", lambda v: v.tensor_scalar_mul(self.AUS[:], self.AUS[:], rz),
                  reads=("PL", "SML/rz"), writes=("PL",))
            for j0 in range(0, 16, 4):
                bk = (j0 // 4) % 2
                kb.mm([(PSF[:, bk, jj * 128:(jj + 1) * 128],
                        self.AUS[:, (j0 + jj) * 128:(j0 + jj + 1) * 128],
                        self.CID[:], jj == 0, jj == 3, dict(is_transpose=True))
                       for jj in range(4)],
                      reads=("PL", "CID"), writes=("PB%d" % bk,))
                kb.do("V", lambda v, bk=bk, j0=j0: v.tensor_copy(
                    self.ATR[:, j0 * 128:(j0 + 4) * 128], PSF[:, bk, 0:512]),
                      reads=("PB%d" % bk,), writes=("ATR",))
            for ec in range(C):
                bk = 6 + ec % 2
                kb.mm([(PSF[:, bk, 0:128], XBt[:, kt, ec * 128:(ec + 1) * 128],
                        self.ATR[:, kt * 128:(kt + 1) * 128],
                        kt == 0, kt == NTT - 1) for kt in range(NTT)],
                      reads=("ATR",) + xbwr, writes=("PB%d" % bk,))
                kb.do("V", lambda v, ec=ec, qt=qt, bk=bk: v.tensor_add(
                    ST[:, ec, qt * 128:(qt + 1) * 128],
                    ST[:, ec, qt * 128:(qt + 1) * 128], PSF[:, bk, 0:128]),
                      reads=("PB%d" % bk,), writes=("ST/%d" % (qt // 4),))

        for tg in range(NTG):
            sl = slice(tg * TGS, (tg + 1) * TGS)
            kb.do("V", lambda v, sl=sl: v.tensor_copy(XB[:, :, sl], ST[:, :, sl]),
                  reads=("ST/%d" % tg,), writes=("XB/%d" % tg,))
            self._ln_gb(tg, XB, 16, 24, "SPR/l1w", "SPR/l1b", self.CF[:, 0:1])
            self._ln_gb(tg, XB, 32, 40, "SPR/l2w", "SPR/l2b", self.CF[:, 0:1])

        w1r = [self.WAR[:, 0:1024].rearrange("p (c f) -> p c f", f=128),
               self.WAR[:, 1024:2048].rearrange("p (c f) -> p c f", f=128)]
        w2ha = self.WAR[:, 2048:4096].rearrange("p (c f) -> p c f", f=128)
        w2hb = self.WAR[:, 4096:6144].rearrange("p (c f) -> p c f", f=128)
        YBh = self.YB.rearrange("p (a b) -> p a b", b=1024)
        ZBh = self.ZB.rearrange("p (a b) -> p a b", b=1024)
        kb.retag("YB")
        kb.retag("ZB")
        for th in range(2):
            hsl = slice(th * 1024, (th + 1) * 1024)
            for fc in range(32):
                ring = fc % 2
                kb.dma(self.cW, "G", w1r[ring][:],
                       io["w_l1a_1"].ap()[l, :, :, fc * 128:(fc + 1) * 128],
                       writes=("WAR/F%d" % ring,))
                hv = YBh if fc < 16 else ZBh
                pfx = "YB" if fc < 16 else "ZB"
                hreg = "%s/%d/%d" % (pfx, (fc % 16) // 2, fc % 2)
                for g2 in range(2):
                    sl = slice(th * 1024 + g2 * 512, th * 1024 + (g2 + 1) * 512)
                    bk = (fc + g2) % 4
                    kb.mm([(PSF[:, bk, :], w1r[ring][:, ic, :], XB[:, ic, sl],
                            ic == 0, ic == C - 1) for ic in range(C)],
                          reads=("WAR/F%d" % ring,
                                 "XB/%d" % (th * 2 + g2)),
                          writes=("PB%d" % bk,))
                    kb.do("S", lambda s, hv=hv, fc=fc, g2=g2, bk=bk:
                          s.activation(hv[:, fc % 16, g2 * 512:(g2 + 1) * 512],
                                       PSF[:, bk, :], AF.Relu,
                                       bias=self.SPR[:, 64 + fc:65 + fc]),
                          reads=("PB%d" % bk, "SPR/b1"), writes=(hreg,))
            hregs = tuple("YB/%d" % i for i in range(8)) + \
                tuple("ZB/%d" % i for i in range(8))
            for oc2 in range(C):
                kb.dma(self.cW, "G", w2ha[:],
                       io["w_l1a_2"].ap()[l, :, 0:16, oc2 * 128:(oc2 + 1) * 128],
                       writes=("WAR/GA",))
                kb.dma(self.cW, "G", w2hb[:],
                       io["w_l1a_2"].ap()[l, :, 16:32, oc2 * 128:(oc2 + 1) * 128],
                       writes=("WAR/GB",))
                for g2 in range(2):
                    sl = slice(th * 1024 + g2 * 512, th * 1024 + (g2 + 1) * 512)
                    bk = 4 + (oc2 * 2 + g2) % 4
                    mms = []
                    for fc in range(32):
                        hv = YBh if fc < 16 else ZBh
                        wsl = w2ha if fc < 16 else w2hb
                        mms.append((PSF[:, bk, :], wsl[:, fc % 16, :],
                                    hv[:, fc % 16, g2 * 512:(g2 + 1) * 512],
                                    fc == 0, fc == 31))
                    kb.mm(mms, reads=("WAR/GA", "WAR/GB") + hregs,
                          writes=("PB%d" % bk,))
                    kb.do("V", lambda v, oc2=oc2, sl=sl, bk=bk:
                          v.scalar_tensor_tensor(
                              ST[:, oc2, sl], PSF[:, bk, :],
                              self.SPR[:, 96 + oc2:97 + oc2], ST[:, oc2, sl],
                              ALU.add, ALU.add),
                          reads=("PB%d" % bk, "SPR/b2"),
                          writes=("ST/%d" % (th * 2 + g2),))

        for tg in range(NTG):
            sl = slice(tg * TGS, (tg + 1) * TGS)
            kb.do("V", lambda v, sl=sl: v.tensor_copy(XB[:, :, sl], ST[:, :, sl]),
                  reads=("ST/%d" % tg,), writes=("XB/%d" % tg,))
            self._ln_gb(tg, XB, 48, 56, "SPR/l3w", "SPR/l3b", self.CF[:, 0:1])


def build_nc(shapes, nblocks):
    nc = bass.Bass("TRN2", target_bir_lowering=False)
    p = Prog(nc, nblocks)
    p.build(shapes)
    return nc


def kernel(_nblocks=6, _cores=None, **inputs):
    sh, percore = _stage(inputs)
    cores = list(range(NCORES)) if _cores is None else _cores
    shapes = dict(sh)
    shapes.update(percore[0])
    nc = build_nc(shapes, _nblocks)
    in_maps = []
    for ci in cores:
        m = dict(sh)
        m.update(percore[ci])
        in_maps.append(m)
    res = run_bass_kernel_spmd(nc, in_maps, core_ids=list(range(len(cores))))
    out = np.empty((NCORES, T, D), np.float32)
    for i, ci in enumerate(cores):
        xo = res.results[i]["x_out"]
        out[ci] = np.asarray(xo, np.float32).transpose(2, 1, 0).reshape(T, D)
    return out
